# version 22
# baseline (speedup 1.0000x reference)
import contextlib
import numpy as np
import ml_dtypes

import concourse.bass as bass
import concourse.mybir as mybir
from concourse.bass_utils import run_bass_kernel_spmd

F32 = mybir.dt.float32
F16 = mybir.dt.float16
BF16 = mybir.dt.bfloat16
FP8 = mybir.dt.float8e4
ALU = mybir.AluOpType
AF = mybir.ActivationFunctionType
AX = mybir.AxisListType

D = 1024
DI = 2048
H4 = 4096
NST = 16
T = 2048
TC = 512
NCH = T // TC
NDT = D // 128
NDI = DI // 128
NH4 = H4 // 128
HEADS = 16
DH = 64
SUB = 128
NSUB = TC // SUB
EPS_LN = 1e-5

ENGINES = ("sync", "tensor", "vector", "scalar", "gpsimd")
BF = ml_dtypes.bfloat16


class Ctr:
    __slots__ = ("h", "n", "name")

    def __init__(self, h, name):
        self.h = h
        self.n = 0
        self.name = name


class Plan:
    def __init__(self):
        self.st = {e: [] for e in ENGINES}

    def w(self, eng, ctr, val=None):
        v = ctr.n if val is None else val
        assert v <= ctr.n, f"wait on future {ctr.name}: {v} > {ctr.n}"
        if v <= 0:
            return
        self.st[eng].append(lambda E, h=ctr.h, v=v: E.wait_ge(h, v))

    def o(self, eng, fn, inc=None, k=1):
        if inc is None:
            self.st[eng].append(fn)
            return None
        inc.n += k
        self.st[eng].append(lambda E, f=fn, h=inc.h, kk=k: f(E).then_inc(h, kk))
        return inc.n

    def dma(self, eng, out, in_, inc):
        return self.o(eng, lambda E, o=out, i=in_: E.dma_start(out=o, in_=i),
                      inc, 16)

    def emit(self, nc):
        with nc.Block() as blk:
            for eng in ENGINES:
                ops = self.st[eng]
                if not ops:
                    continue
                getattr(blk, eng)(lambda E, ops=ops: [f(E) for f in ops])


def slc(ap, i, w):
    return ap[:, i * w:(i + 1) * w]


def build_program(phases=("P1", "P2", "P4", "P5", "P6"), debug_out=()):
    nc = bass.Bass("TRN2", target_bir_lowering=False)
    stack = contextlib.ExitStack()
    dt = {}

    def din(name, shape, dtype=F32):
        dt[name] = nc.dram_tensor(name, shape, dtype, kind="ExternalInput").ap()

    din("xin", [D, T])
    din("w_in_blk", [NH4 * 128, 8 * 128])
    din("w_delta_blk", [NDI * 128, NDI * 128])
    din("w_out_blk", [NDT * 128, NDI * 128])
    din("w_b_t", [128, NDI * NST])
    din("w_c_t", [128, NDI * NST])
    din("a_log_t", [128, NDI * NST])
    din("b_delta_t", [128, NDI])
    din("d_param_t", [128, NDI])
    din("w1_t", [D, H4])
    din("w2_t", [H4, D])
    din("b1_t", [128, NH4])
    din("b2_t", [128, NDT])
    din("wq_t", [D, D])
    din("wk_t", [D, D])
    din("wv_t", [D, D])
    din("wo_t", [D, D])
    din("bq_t", [128, NDT])
    din("bk_t", [128, NDT])
    din("bv_row", [1, D])
    din("bo_t", [128, NDT])
    din("dropm", [D, T], BF16)

    out_names = set(debug_out)

    def scratch(name, shape, dtype):
        kind = "ExternalOutput" if name in out_names else "Internal"
        dt[name] = nc.dram_tensor(name, shape, dtype, kind=kind).ap()

    scratch("sres", [DI, T], BF16)
    scratch("mout", [D, T], BF16)
    scratch("x1", [D, T], F32)
    scratch("x2", [D, T], F32)
    if "u_dbg" in out_names:
        scratch("u_dbg", [DI, T], BF16)
    if "del_dbg" in out_names:
        scratch("del_dbg", [DI, T], F16)
    if "xn8_dbg" in out_names:
        scratch("xn8_dbg", [D, T], FP8)
    if "h8_dbg" in out_names:
        scratch("h8_dbg", [H4, T], FP8)
    if "fq_dbg" in out_names:
        scratch("fq_dbg", [D, T], BF16)
        scratch("fk_dbg", [D, T], BF16)
    if "vtm_dbg" in out_names:
        scratch("vtm_dbg", [T, D], BF16)
    if "den_dbg" in out_names:
        scratch("den_dbg", [T, HEADS], F32)
    if "aos_dbg" in out_names:
        scratch("aos_dbg", [T, D], BF16)
    if "aot_dbg" in out_names:
        scratch("aot_dbg", [128, NCH * NDT * TC], BF16)
    if "wo_dbg" in out_names:
        scratch("wo_dbg", [128, 8 * D], BF16)
    if "my_dbg" in out_names:
        scratch("my_dbg", [DI, T], BF16)
    if "bc_dbg" in out_names:
        scratch("bc_dbg", [32, T], BF16)
    dt["out"] = nc.dram_tensor("out", [D, T], F32, kind="ExternalOutput").ap()

    sel = np.zeros((16, NST * 128), np.float32)
    for n in range(NST):
        sel[n, n * 128:(n + 1) * 128] = 1.0
    const_np = {
        "cvals": np.broadcast_to(
            np.array([EPS_LN, 1.0], np.float32), (128, 2)).copy(),
        "ones_col_bf": np.ones((128, 1), np.float32).astype(BF),
        "ones_row_f32": np.ones((1, 128), np.float32),
        "ones_row_bf": np.ones((1, 128), np.float32).astype(BF),
        "ident_bf": np.eye(128, dtype=np.float32).astype(BF),
        "cmask_bf": np.triu(np.ones((128, 128), np.float32)).astype(BF),
        "sel_bf": sel.astype(BF),
    }
    const_dt = {"float32": F32, "bfloat16": BF16}

    sems = {}
    for s in ("sv", "ss", "st", "sg", "dw", "dx", "do", "dm"):
        sems[s] = Ctr(stack.enter_context(nc.semaphore(s)), s)
    sv, ss, st, sg = sems["sv"], sems["ss"], sems["st"], sems["sg"]
    dw, dx, do, dm = sems["dw"], sems["dx"], sems["do"], sems["dm"]

    cons = {}
    p0 = Plan()
    for nm, arr in const_np.items():
        dram = nc.inline_tensor(arr, name="ci_" + nm).ap()
        sb = _sbp(nc, stack, "c_" + nm, list(arr.shape),
                  const_dt[str(arr.dtype)])
        cons[nm] = sb
        p0.dma("sync", sb[:, :], dram[:, :], dw)
    for eng in ENGINES:
        p0.w(eng, dw)
    p0.emit(nc)

    deps = {"mout_c": [0] * NCH, "x1_c": [0] * NCH, "x2_c": [0] * NCH}
    mstack = contextlib.ExitStack()
    u = _sbp(nc, mstack, "u", [128, NDI * T], BF16)

    def uslc(ki, c):
        return u[:, ki * T + c * TC: ki * T + (c + 1) * TC]

    if "P1" in phases:
        with contextlib.ExitStack() as ph:
            p = Plan()
            xn = _sbp(nc, ph, "p1_xn", [128, NDT * T], BF16)
            xin_b = _sbp(nc, ph, "p1_xin", [128, 2 * NDT * TC], F32)
            wstage = _sbp(nc, ph, "p1_wst", [128, 2 * 1024], F32)
            wbf = _sbp(nc, ph, "p1_wbf", [128, 2 * 1024], BF16)
            evb = _sbp(nc, ph, "p1_ev", [128, 2 * TC], BF16)
            lnb = alloc_ln_bufs(nc, ph, TC, "p1")
            ps_u = _psp(nc, ph, "p1_psu", [128, 2 * TC])

            xin_r = dt["xin"].rearrange("(a p) t -> p a t", p=128)
            xn_done = [0] * NCH
            for c in range(NCH):
                buf = slc(xin_b, c % 2, NDT * TC)
                if c >= 2:
                    p.w("sync", sv, xn_done[c - 2])
                p.dma("sync", buf.rearrange("p (a t) -> p a t", a=NDT),
                      xin_r[:, :, c * TC:(c + 1) * TC], dx)
                p.w("vector", dx)
                p.w("scalar", dx)
                xn_done[c] = emit_ln(
                    p, sems, buf,
                    lambda a, c=c: xn[:, a * T + c * TC: a * T + (c + 1) * TC],
                    TC, cons, lnb)
            cast_done = [0] * NH4
            mm_done = [0] * NH4
            evac_done = {}
            res_dma = {}
            p.w("tensor", sv, xn_done[NCH - 1])
            for m in range(NH4):
                par = m % 2
                if m >= 2:
                    p.w("sync", ss, cast_done[m - 2])
                p.dma("sync", slc(wstage, par, 1024),
                      dt["w_in_blk"][m * 128:(m + 1) * 128, :], dw)
                p.w("scalar", dw)
                if m >= 2:
                    p.w("scalar", st, mm_done[m - 2])
                p.o("scalar", lambda E, par=par: E.copy(
                    slc(wbf, par, 1024), slc(wstage, par, 1024)), ss)
                cast_done[m] = ss.n
                p.w("tensor", ss, cast_done[m])
                for c in range(NCH):
                    idx = m * NCH + c
                    pp = idx % 2
                    if idx >= 2:
                        p.w("tensor", ss, evac_done[idx - 2])
                    for k in range(8):
                        p.o("tensor", lambda E, par=par, k=k, c=c, pp=pp:
                            E.matmul(slc(ps_u, pp, TC),
                                     wbf[:, par * 1024 + k * 128:
                                         par * 1024 + (k + 1) * 128],
                                     xn[:, k * T + c * TC: k * T + (c + 1) * TC],
                                     start=(k == 0), stop=(k == 7)),
                            st if k == 7 else None)
                    mm_done[m] = st.n
                    p.w("scalar", st, mm_done[m])
                    if m < NDI:
                        p.o("scalar", lambda E, m=m, c=c, pp=pp: E.activation(
                            uslc(m, c), slc(ps_u, pp, TC), AF.Silu), ss)
                        evac_done[idx] = ss.n
                    else:
                        ridx = (m - NDI) * NCH + c
                        rp = ridx % 2
                        if ridx >= 2:
                            p.w("scalar", do, res_dma[ridx - 2])
                        p.o("scalar", lambda E, rp=rp, pp=pp: E.activation(
                            slc(evb, rp, TC), slc(ps_u, pp, TC), AF.Silu), ss)
                        evac_done[idx] = ss.n
                        res_dma[ridx] = p.dma(
                            "scalar",
                            dt["sres"][(m - NDI) * 128:(m - NDI + 1) * 128,
                                       c * TC:(c + 1) * TC],
                            slc(evb, rp, TC), do)
            if "u_dbg" in out_names:
                p.w("gpsimd", ss)
                for ki in range(NDI):
                    p.dma("gpsimd", dt["u_dbg"][ki * 128:(ki + 1) * 128, :],
                          u[:, ki * T:(ki + 1) * T], dm)
                p.w("gpsimd", dm)
            p.emit(nc)
    u_ready = ss.n

    if "P2" in phases:
        with contextlib.ExitStack() as ph:
            p = Plan()
            kw_st = _sbp(nc, ph, "p2_kwst", [128, 3 * 2048], F32)
            kw_bf = _sbp(nc, ph, "p2_kwbf", [128, 3 * 2048], BF16)
            wb_sb = _sbp(nc, ph, "p2_wb", [128, NDI * NST], BF16)
            wc_sb = _sbp(nc, ph, "p2_wc", [128, NDI * NST], BF16)
            a_sb = _sbp(nc, ph, "p2_a", [128, NDI * NST], F32)
            bd_sb = _sbp(nc, ph, "p2_bd", [128, NDI], F32)
            dp_sb = _sbp(nc, ph, "p2_dp", [128, NDI], F32)
            B_sb = _sbp(nc, ph, "p2_B", [16, TC], BF16)
            C_sb = _sbp(nc, ph, "p2_C", [16, TC], BF16)
            Bbc = _sbp(nc, ph, "p2_Bbc", [128, NST * TC], BF16)
            Cbc = _sbp(nc, ph, "p2_Cbc", [128, NST * TC], BF16)
            d16 = _sbp(nc, ph, "p2_d16", [128, TC], F16)
            dub = _sbp(nc, ph, "p2_du", [128, TC], BF16)
            sp_e = _sbp(nc, ph, "p2_spe", [128, TC], F32)
            sp_t1 = _sbp(nc, ph, "p2_spt1", [128, TC], F32)
            sp_t2 = _sbp(nc, ph, "p2_spt2", [128, TC], F32)
            dab = _sbp(nc, ph, "p2_dab", [128, 4 * TC], F32)
            bb = _sbp(nc, ph, "p2_bb", [128, 4 * TC], BF16)
            hs = _sbp(nc, ph, "p2_hs", [128, 4 * TC], BF16)
            hcb = _sbp(nc, ph, "p2_hc", [128, 4 * TC], BF16)
            yp = _sbp(nc, ph, "p2_yp", [128, TC], F32)
            yp12 = _sbp(nc, ph, "p2_yp12", [128, 2 * TC], F32)
            yd = _sbp(nc, ph, "p2_yd", [128, TC], F32)
            hstate = _sbp(nc, ph, "p2_hst", [128, NDI * NST], F32)
            myb = _sbp(nc, ph, "p2_my", [128, NDI * TC], BF16)
            ost = _sbp(nc, ph, "p2_ost", [128, 2 * TC], BF16)
            srb = _sbp(nc, ph, "p2_srb", [128, 2 * TC], BF16)
            ps_d = _psp(nc, ph, "p2_psd", [128, 2 * TC])
            ps_sb = _psp(nc, ph, "p2_pssb", [16, TC])
            ps_sc = _psp(nc, ph, "p2_pssc", [16, TC])
            ps_br = _psp(nc, ph, "p2_psbr", [128, 2 * TC])
            ps_o = _psp(nc, ph, "p2_pso", [128, 2 * TC])

            p.w("sync", ss, u_ready)
            p.dma("sync", kw_st[:, 0:256], dt["w_b_t"][:, :], dw)
            p.dma("sync", kw_st[:, 256:512], dt["w_c_t"][:, :], dw)
            p.dma("sync", kw_st[:, 512:768], dt["a_log_t"][:, :], dw)
            p.dma("sync", bd_sb[:, :], dt["b_delta_t"][:, :], dw)
            p.dma("sync", dp_sb[:, :], dt["d_param_t"][:, :], dw)
            p.w("vector", dw)
            p.o("vector", lambda E: E.tensor_copy(wb_sb[:, :], kw_st[:, 0:256]))
            p.o("vector", lambda E: E.tensor_copy(wc_sb[:, :], kw_st[:, 256:512]))
            p.o("vector", lambda E: E.memset(hstate[:, :], 0.0), sv)
            p.w("scalar", dw)
            p.o("scalar", lambda E: E.activation(
                sp_e[:, 0:256], kw_st[:, 512:768], AF.Exp), ss)
            p.w("vector", ss)
            p.o("vector", lambda E: E.tensor_scalar_mul(
                a_sb[:, :], sp_e[:, 0:256], -1.0), sv)
            prep_done_v = sv.n
            prep_done_s = ss.n
            p.w("tensor", sv, prep_done_v)
            p.w("tensor", ss, u_ready)

            kw_cast = {}
            kw_mmdone = {}
            bc_evac_c = [0] * NCH
            my_done_c = [0] * NCH
            wout_done_c = [0] * NCH
            bsb_evac = [0] * NCH
            ost_dma = {}
            out_evac = {}
            sres_dma = {}
            psd_free = {}
            dbg_del = {}
            scan_hist = []

            def kw_block(bi, src_rows):
                par = bi % 3
                if bi >= 3:
                    p.w("sync", ss, kw_cast[bi - 3])
                else:
                    p.w("sync", sv, prep_done_v)
                    p.w("sync", ss, prep_done_s)
                p.dma("sync", slc(kw_st, par, 2048), src_rows, dw)
                p.w("scalar", dw)
                if bi >= 3:
                    p.w("scalar", st, kw_mmdone[bi - 3])
                p.o("scalar", lambda E, par=par: E.copy(
                    slc(kw_bf, par, 2048), slc(kw_st, par, 2048)), ss)
                kw_cast[bi] = ss.n

            def sres_load(dc, c):
                i = c * NDI + dc
                if i >= 2:
                    p.w("scalar", sv, my_dc_done.get(i - 2, 0))
                sres_dma[i] = p.dma(
                    "scalar", slc(srb, dc % 2, TC),
                    dt["sres"][dc * 128:(dc + 1) * 128, c * TC:(c + 1) * TC], dm)

            my_dc_done = {}

            def blk_src(bj):
                cj, rj = divmod(bj, NDI + NDT)
                if rj < NDI:
                    return dt["w_delta_blk"][rj * 128:(rj + 1) * 128, :]
                return dt["w_out_blk"][(rj - NDI) * 128:(rj - NDI + 1) * 128, :]

            def emit_wout(c):
                p.w("tensor", sv, my_done_c[c])
                for mt in range(NDT):
                    bi = c * (NDI + NDT) + NDI + mt
                    if bi + 1 < NCH * (NDI + NDT):
                        kw_block(bi + 1, blk_src(bi + 1))
                    opar = mt % 2
                    p.w("tensor", ss, kw_cast[bi])
                    oi = c * NDT + mt
                    if oi >= 2:
                        p.w("tensor", ss, out_evac[oi - 2])
                    for k in range(NDI):
                        p.o("tensor", lambda E, bip=bi % 3, k=k, opar=opar:
                            E.matmul(slc(ps_o, opar, TC),
                                     kw_bf[:, bip * 2048 + k * 128:
                                           bip * 2048 + (k + 1) * 128],
                                     slc(myb, k, TC), start=(k == 0),
                                     stop=(k == NDI - 1)),
                            st if k == NDI - 1 else None)
                    kw_mmdone[bi] = st.n
                    p.w("scalar", st, kw_mmdone[bi])
                    if oi >= 2:
                        p.w("scalar", do, ost_dma[oi - 2])
                    p.o("scalar", lambda E, opar=opar: E.copy(
                        slc(ost, opar, TC), slc(ps_o, opar, TC)), ss)
                    out_evac[oi] = ss.n
                    ost_dma[oi] = p.dma(
                        "scalar",
                        dt["mout"][mt * 128:(mt + 1) * 128, c * TC:(c + 1) * TC],
                        slc(ost, opar, TC), do)
                wout_done_c[c] = st.n
                deps["mout_c"][c] = do.n

            for c in range(NCH):
                if c > 0:
                    p.w("tensor", ss, bsb_evac[c - 1])
                    if ("my_dbg" in out_names or "bc_dbg" in out_names):
                        p.w("scalar", dm)
                        p.w("vector", dm)
                for k in range(NDI):
                    p.o("tensor", lambda E, k=k, c=c: E.matmul(
                        ps_sb[:, :], slc(wb_sb, k, NST), uslc(k, c),
                        start=(k == 0), stop=(k == NDI - 1)))
                for k in range(NDI):
                    p.o("tensor", lambda E, k=k, c=c: E.matmul(
                        ps_sc[:, :], slc(wc_sb, k, NST), uslc(k, c),
                        start=(k == 0), stop=(k == NDI - 1)),
                        st if k == NDI - 1 else None)
                p.w("scalar", st)
                p.o("scalar", lambda E: E.copy(B_sb[:, :], ps_sb[:, :]), ss)
                p.o("scalar", lambda E: E.copy(C_sb[:, :], ps_sc[:, :]), ss)
                bsb_evac[c] = ss.n
                p.w("tensor", ss, bsb_evac[c])
                bc_ev = {}
                for i in range(2 * NST):
                    n = i % NST
                    src = B_sb if i < NST else C_sb
                    dst = Bbc if i < NST else Cbc
                    bpar = i % 2
                    if i >= 2:
                        p.w("tensor", sv, bc_ev[i - 2])
                    elif c > 0:
                        p.w("tensor", sv, bc_evac_c[c - 1])
                    p.o("tensor", lambda E, n=n, src=src, bpar=bpar: E.matmul(
                        slc(ps_br, bpar, TC), cons["sel_bf"][:, n * 128:(n + 1) * 128],
                        src[:, :], start=True, stop=True), st)
                    p.w("vector", st)
                    p.o("vector", lambda E, n=n, dst=dst, bpar=bpar:
                        E.tensor_copy(slc(dst, n, TC), slc(ps_br, bpar, TC)),
                        sv)
                    bc_ev[i] = sv.n
                bc_evac_c[c] = sv.n
                sres_load(0, c)
                if c > 0:
                    emit_wout(c - 1)

                if c == 0:
                    kw_block(0, blk_src(0))
                for dc in range(NDI):
                    bi = c * (NDI + NDT) + dc
                    kw_block(bi + 1, blk_src(bi + 1))
                    if dc + 1 < NDI:
                        sres_load(dc + 1, c)
                    par = dc % 2
                    p.w("tensor", ss, kw_cast[bi])
                    if bi >= 2:
                        p.w("tensor", sv, psd_free.get(bi - 2, 0))
                    for k in range(NDI):
                        p.o("tensor", lambda E, par=par, bip=bi % 3, k=k, c=c:
                            E.matmul(slc(ps_d, par, TC),
                                     kw_bf[:, bip * 2048 + k * 128:
                                           bip * 2048 + (k + 1) * 128],
                                     uslc(k, c), start=(k == 0),
                                     stop=(k == NDI - 1)),
                            st if k == NDI - 1 else None)
                    kw_mmdone[bi] = st.n
                    dmm = st.n
                    p.w("vector", st, dmm)
                    p.o("vector", lambda E, par=par, dc=dc: E.tensor_scalar(
                        sp_e[:, :], slc(ps_d, par, TC), bd_sb[:, dc:dc + 1],
                        None, ALU.add), sv)
                    psd_free[bi] = sv.n
                    p.o("vector", lambda E: E.scalar_tensor_tensor(
                        sp_t1[:, :], sp_e[:, :], -1.0, sp_e[:, :],
                        ALU.mult, ALU.max))
                    p.o("vector", lambda E: E.tensor_single_scalar(
                        sp_e[:, :], sp_e[:, :], 0.0, ALU.max), sv)
                    v_abs = sv.n
                    p.w("scalar", sv, v_abs)
                    p.o("scalar", lambda E: E.activation(
                        sp_t2[:, :], sp_t1[:, :], AF.Exp, scale=-1.0))
                    p.o("scalar", lambda E: E.activation(
                        sp_t1[:, :], sp_t2[:, :], AF.Ln,
                        bias=cons["cvals"][:, 1:2]), ss)
                    s_ln = ss.n
                    p.w("vector", ss, s_ln)
                    if "del_dbg" in out_names and (c, dc) != (0, 0):
                        pk = (c, dc - 1) if dc > 0 else (c - 1, NDI - 1)
                        p.w("vector", dm, dbg_del[pk])
                    p.o("vector", lambda E: E.tensor_add(
                        d16[:, :], sp_t1[:, :], sp_e[:, :]), sv)
                    v_d16 = sv.n
                    p.o("vector", lambda E, dc=dc, c=c: E.tensor_mul(
                        dub[:, :], d16[:, :], uslc(dc, c)), sv)
                    v_du = sv.n
                    if "del_dbg" in out_names:
                        p.w("gpsimd", sv, v_du)
                        dbg_del[(c, dc)] = p.dma(
                            "gpsimd",
                            dt["del_dbg"][dc * 128:(dc + 1) * 128,
                                          c * TC:(c + 1) * TC],
                            d16[:, :], dm)
                    for n in range(NST):
                        gi = len(scan_hist)
                        apar = gi % 4
                        bpar = gi % 4
                        if n % 2 == 0:
                            p.w("scalar", sv,
                                max(v_d16,
                                    scan_hist[gi - 3] if gi >= 3 else 0))
                            for nn in (n, n + 1):
                                p.o("scalar",
                                    lambda E, apar=(gi + nn - n) % 4, dc=dc,
                                    nn=nn: E.activation(
                                        slc(dab, apar, TC), d16[:, :], AF.Exp,
                                        scale=a_sb[:, dc * NST + nn:
                                                   dc * NST + nn + 1]),
                                    ss if nn % 2 == 1 else None)
                            da_done = ss.n
                        p.o("vector", lambda E, bpar=bpar, n=n: E.tensor_mul(
                            slc(bb, bpar, TC), dub[:, :], slc(Bbc, n, TC)))
                        if n % 2 == 0:
                            p.w("vector", ss, da_done)
                        p.o("vector", lambda E, apar=apar, bpar=bpar, dc=dc, n=n:
                            E.tensor_tensor_scan(
                                slc(hs, bpar, TC), slc(dab, apar, TC),
                                slc(bb, bpar, TC),
                                hstate[:, dc * NST + n: dc * NST + n + 1],
                                ALU.mult, ALU.add),
                            sv if n % 2 == 1 else None)
                        scan_hist.append(sv.n)
                        if n % 2 == 1:
                            lo = (gi - 1) % 4
                            p.o("vector", lambda E, lo=lo, n=n: E.tensor_mul(
                                hcb[:, lo * TC:(lo + 2) * TC],
                                hs[:, lo * TC:(lo + 2) * TC],
                                Cbc[:, (n - 1) * TC:(n + 1) * TC]))
                        if n % 4 == 3:
                            grp = n // 4
                            hsr = hs.rearrange("p (n t) -> p n t", n=4)
                            p.o("vector", lambda E, dc=dc, n=n, hsr=hsr:
                                E.tensor_copy(
                                    hstate[:, dc * NST + n - 3:
                                           dc * NST + n + 1],
                                    hsr[:, :, TC - 1:TC].rearrange(
                                        "p n t -> p (n t)")))
                            p.o("vector", lambda E: E.tensor_add(
                                yp12[:, :], hcb[:, 0:2 * TC],
                                hcb[:, 2 * TC:4 * TC]))
                            if grp == 0:
                                p.o("vector", lambda E: E.tensor_add(
                                    yd[:, :], yp12[:, 0:TC], yp12[:, TC:2 * TC]),
                                    sv)
                            else:
                                p.o("vector", lambda E: E.tensor_add(
                                    yp[:, :], yp12[:, 0:TC], yp12[:, TC:2 * TC]))
                                p.o("vector", lambda E: E.tensor_add(
                                    yd[:, :], yd[:, :], yp[:, :]), sv)
                    if dc == 0 and c > 0:
                        p.w("vector", st, wout_done_c[c - 1])
                    p.o("vector", lambda E, dc=dc, c=c: E.tensor_scalar_mul(
                        dub[:, :], uslc(dc, c), dp_sb[:, dc:dc + 1]))
                    p.o("vector", lambda E: E.tensor_add(
                        sp_e[:, :], yd[:, :], dub[:, :]))
                    p.w("vector", dm, sres_dma[c * NDI + dc])
                    p.o("vector", lambda E, dc=dc: E.tensor_mul(
                        slc(myb, dc, TC), sp_e[:, :], slc(srb, dc % 2, TC)), sv)
                    my_dc_done[c * NDI + dc] = sv.n
                my_done_c[c] = sv.n
                if "my_dbg" in out_names:
                    p.w("gpsimd", sv, my_done_c[c])
                    for dcq in range(NDI):
                        p.dma("gpsimd",
                              dt["my_dbg"][dcq * 128:(dcq + 1) * 128,
                                           c * TC:(c + 1) * TC],
                              slc(myb, dcq, TC), dm)
                if "bc_dbg" in out_names:
                    p.w("gpsimd", sv, bc_evac_c[c])
                    p.dma("gpsimd", dt["bc_dbg"][0:16, c * TC:(c + 1) * TC],
                          B_sb[:, :], dm)
                    p.dma("gpsimd", dt["bc_dbg"][16:32, c * TC:(c + 1) * TC],
                          C_sb[:, :], dm)
                dbg_dm_c = dm.n
            emit_wout(NCH - 1)
            p.w("gpsimd", do)
            p.emit(nc)
    mstack.close()

    for PH, xin_src, second_in, xout_dst in (
            ("P4", "xin", "mout", "x1"),
            ("P6", "x2", None, "out")):
        if PH not in phases:
            continue
        emit_ffn(nc, dt, cons, sems, deps, PH, xin_src, second_in, xout_dst)

        if PH == "P4" and "P5" in phases:
            emit_attn(nc, dt, cons, sems, deps)

    stack.close()
    return nc, dt


def _sbp(nc, st, name, shape, dtype):
    return st.enter_context(nc.sbuf_tensor(name, shape, dtype)).ap()


def _psp(nc, st, name, shape, dtype=F32):
    return st.enter_context(nc.psum_tensor(name, shape, dtype)).ap()


def alloc_ln_bufs(nc, st, W, tag):
    return {
        "xb": _sbp(nc, st, f"ln_xb_{tag}", [128, NDT * W], BF16),
        "xsq": _sbp(nc, st, f"ln_xsq_{tag}", [128, NDT * W], BF16),
        "t": _sbp(nc, st, f"ln_t_{tag}", [128, W], F32),
        "srow": _sbp(nc, st, f"ln_srow_{tag}", [1, 6 * W], F32),
        "sum": _psp(nc, st, f"ln_sum_{tag}", [1, W]),
        "ssq": _psp(nc, st, f"ln_ssq_{tag}", [1, W]),
        "bc": _psp(nc, st, f"ln_bc_{tag}", [128, 2 * W]),
    }


def emit_ln(p, sems, x_ap, dst, W, cons, b, psum_guard=None, x_gate=None):
    sv, ss, st = sems["sv"], sems["ss"], sems["st"]
    if x_gate is not None:
        p.w("vector", x_gate[0], x_gate[1])
        p.w("scalar", x_gate[0], x_gate[1])

    def sr(i):
        return b["srow"][0:1, i * W:(i + 1) * W]
    if psum_guard is not None:
        for gc, gvv in psum_guard:
            p.w("tensor", gc, gvv)
    for a in range(NDT):
        p.o("vector", lambda E, a=a: E.tensor_copy(
            slc(b["xb"], a, W), slc(x_ap, a, W)), sv)
        p.o("scalar", lambda E, a=a: E.square(
            slc(b["xsq"], a, W), slc(x_ap, a, W)), ss)
        p.w("tensor", sv)
        p.w("tensor", ss)
        p.o("tensor", lambda E, a=a: E.matmul(
            b["sum"][:, :], cons["ones_col_bf"][:, :], slc(b["xb"], a, W),
            start=(a == 0), stop=(a == NDT - 1)))
        p.o("tensor", lambda E, a=a: E.matmul(
            b["ssq"][:, :], cons["ones_col_bf"][:, :], slc(b["xsq"], a, W),
            start=(a == 0), stop=(a == NDT - 1)),
            st if a == NDT - 1 else None)
    v_cast, s_sq = sv.n, ss.n
    t_stats = st.n
    p.w("vector", st, t_stats)
    p.o("vector", lambda E: E.tensor_scalar_mul(sr(0), b["sum"][0:1, :], 1.0 / D))
    p.o("vector", lambda E: E.tensor_mul(sr(1), sr(0), sr(0)))
    p.o("vector", lambda E: E.scalar_tensor_tensor(
        sr(2), b["ssq"][0:1, :], 1.0 / D, sr(1), ALU.mult, ALU.subtract), sv)
    v_var = sv.n
    p.w("scalar", sv, v_var)
    p.o("scalar", lambda E: E.activation(sr(3), sr(2), AF.Ln, bias=cons["cvals"][0:1, 0:1]))
    p.o("scalar", lambda E: E.activation(sr(4), sr(3), AF.Exp, scale=-0.5), ss)
    s_rstd = ss.n
    p.w("vector", ss, s_rstd)
    p.o("vector", lambda E: E.tensor_mul(sr(5), sr(0), sr(4)), sv)
    v_mrs = sv.n
    p.w("tensor", sv, v_mrs)
    p.o("tensor", lambda E: E.matmul(
        b["bc"][:, 0:W], cons["ones_row_f32"][:, :], sr(4), start=True,
        stop=True))
    p.o("tensor", lambda E: E.matmul(
        b["bc"][:, W:2 * W], cons["ones_row_f32"][:, :], sr(5), start=True,
        stop=True), st)
    t_bc = st.n
    p.w("vector", st, t_bc)
    for a in range(NDT):
        p.o("vector", lambda E, a=a: E.tensor_mul(
            b["t"][:, :], slc(x_ap, a, W), b["bc"][:, 0:W]))
        p.o("vector", lambda E, a=a: E.tensor_sub(
            dst(a), b["t"][:, :], b["bc"][:, W:2 * W]), sv)
    return sv.n


def emit_ffn(nc, dt, cons, sems, deps, tag, xin_src, second_in, xout_dst):
    sv, ss, st = sems["sv"], sems["ss"], sems["st"]
    dw, dx, do, dm = sems["dw"], sems["dx"], sems["do"], sems["dm"]
    in_dep_gate = (deps["mout_c"][NCH - 1] if second_in
                   else deps["x2_c"][NCH - 1])
    with contextlib.ExitStack() as ph:
        p = Plan()
        w1t8 = _sbp(nc, ph, f"{tag}_w1", [128, 8 * H4], FP8)
        w2t8 = _sbp(nc, ph, f"{tag}_w2", [128, NH4 * D], FP8)
        b1s = _sbp(nc, ph, f"{tag}_b1", [128, NH4], F32)
        b2s = _sbp(nc, ph, f"{tag}_b2", [128, NDT], F32)
        for geng in ("sync", "scalar", "gpsimd"):
            p.w(geng, do, in_dep_gate)
        p.dma("sync", b1s[:, :], dt["b1_t"][:, :], dw)
        p.dma("sync", b2s[:, :], dt["b2_t"][:, :], dw)
        b_ready = dw.n
        wcast = {}
        with contextlib.ExitStack() as prep:
            stage = _sbp(nc, prep, f"{tag}_wst", [128, 2 * H4], F32)
            QROT = (("sync", dw), ("scalar", do), ("gpsimd", dm))
            for i in range(8 + 8):
                slot = i % 2
                q, qs = QROT[i % 3]
                if i >= 2:
                    p.w(q, wcast[i - 2][0], wcast[i - 2][1])
                if i < 8:
                    srcw = dt["w1_t"][i * 128:(i + 1) * 128, :]
                    dst = w1t8[:, i * H4:(i + 1) * H4]
                else:
                    g = i - 8
                    srcw = dt["w2_t"][g * 512:(g + 1) * 512, :].rearrange(
                        "(k p) d -> p k d", p=128)
                    dst = w2t8[:, g * H4:(g + 1) * H4]
                width = H4
                sview = stage[:, slot * H4: slot * H4 + width]
                p.dma(q, sview.rearrange("p (k d) -> p k d", k=4)
                      if i >= 8 else sview, srcw, qs)
                ceng = "vector" if i % 2 == 0 else "scalar"
                csem = sv if i % 2 == 0 else ss
                p.w(ceng, qs)
                if ceng == "vector":
                    p.o(ceng, lambda E, dst=dst, slot=slot, width=width:
                        E.tensor_copy(dst,
                                      stage[:, slot * H4: slot * H4 + width]),
                        csem)
                else:
                    p.o(ceng, lambda E, dst=dst, slot=slot, width=width:
                        E.copy(dst, stage[:, slot * H4: slot * H4 + width]),
                        csem)
                wcast[i] = (csem, csem.n)
        w8_ready = sv.n
        w8_ready_ss = ss.n
        x0b = _sbp(nc, ph, f"{tag}_x0", [128, 2 * NDT * TC], F32)
        mob = (_sbp(nc, ph, f"{tag}_mo", [128, NDT * TC], BF16)
               if second_in else None)
        dmb = _sbp(nc, ph, f"{tag}_dm", [128, NDT * TC], BF16)
        x_c = (_sbp(nc, ph, f"{tag}_xc", [128, NDT * TC], F32)
               if second_in else None)
        xn8 = _sbp(nc, ph, f"{tag}_xn8", [128, NDT * TC], FP8)
        h8 = _sbp(nc, ph, f"{tag}_h8", [128, NH4 * TC], FP8)
        h16b = _sbp(nc, ph, f"{tag}_h16", [128, 2 * TC], F16)
        y16b = _sbp(nc, ph, f"{tag}_y16", [128, 2 * TC], F16)
        dtmp = _sbp(nc, ph, f"{tag}_dt", [128, TC], F32)
        out_c = _sbp(nc, ph, f"{tag}_oc", [128, NDT * TC], F32)
        lnb = alloc_ln_bufs(nc, ph, TC, tag)
        ps_h = _psp(nc, ph, f"{tag}_psh", [128, 2 * TC])
        ps_y = _psp(nc, ph, f"{tag}_psy", [128, 2 * TC])

        xin_r = dt[xin_src].rearrange("(a p) t -> p a t", p=128)
        mo_r = dt[second_in].rearrange("(a p) t -> p a t", p=128) \
            if second_in else None
        dm_r = dt["dropm"].rearrange("(a p) t -> p a t", p=128)
        out_r = dt[xout_dst].rearrange("(a p) t -> p a t", p=128)
        in_dep = deps["mout_c"] if second_in else deps["x2_c"]

        asm_done = [0] * NCH
        xread_done = [0] * NCH
        drops_done = {}
        outd = [0] * NCH
        for c in range(NCH):
            slot = c % 2
            if c == 0:
                p.w("sync", sv, w8_ready)
                p.w("sync", ss, w8_ready_ss)
            if c >= 2:
                p.w("sync", sv, xread_done[c - 2])
            if xin_src == "x2":
                p.w("sync", do, deps["x2_c"][c])
            p.dma("sync", slc(x0b, slot, NDT * TC).rearrange(
                "p (a t) -> p a t", a=NDT),
                xin_r[:, :, c * TC:(c + 1) * TC], dx)
            if second_in:
                if c >= 1:
                    p.w("sync", sv, asm_done[c - 1])
                p.w("sync", do, deps["mout_c"][c])
                p.dma("sync", mob.rearrange("p (a t) -> p a t", a=NDT),
                      mo_r[:, :, c * TC:(c + 1) * TC], dx)
            if c >= 1:
                p.w("sync", sv, drops_done.get((c - 1, NDT - 1), 0))
            p.dma("sync", dmb.rearrange("p (a t) -> p a t", a=NDT),
                  dm_r[:, :, c * TC:(c + 1) * TC], dx)
            p.w("vector", dx)
            if c >= 1:
                p.w("vector", do, outd[c - 1])
            if second_in:
                for a in range(NDT):
                    p.o("vector", lambda E, a=a, slot=slot: E.tensor_add(
                        slc(x_c, a, TC),
                        x0b[:, slot * NDT * TC + a * TC:
                            slot * NDT * TC + (a + 1) * TC],
                        slc(mob, a, TC)), sv)
                asm_done[c] = sv.n
                x_ap = x_c
                xoff = 0
            else:
                x_ap = x0b
                xoff = slot * NDT * TC
            xgate = (sems["sv"], asm_done[c]) if second_in \
                else (dx, dx.n)
            xn_done = emit_ln(
                p, sems, x_ap[:, xoff:xoff + NDT * TC],
                lambda a: slc(xn8, a, TC), TC, cons, lnb, x_gate=xgate)
            if tag == "P4" and "xn8_dbg" in dt:
                p.w("gpsimd", sv, xn_done)
                for a in range(NDT):
                    p.dma("gpsimd",
                          dt["xn8_dbg"][a * 128:(a + 1) * 128,
                                        c * TC:(c + 1) * TC],
                          slc(xn8, a, TC), dm)
            p.w("tensor", sv, max(xn_done, w8_ready))
            if c == 0:
                p.w("tensor", ss, w8_ready_ss)
            h_evac = {}
            h8c = {}
            for m in range(NH4):
                hp = m % 2
                if m >= 2:
                    p.w("tensor", ss, h_evac[m - 2])
                w1_3 = w1t8.rearrange("p (k h) -> p k h", k=8)
                xn_3 = xn8.rearrange("p (k t) -> p k t", k=8)
                for kk in range(4):
                    p.o("tensor", lambda E, m=m, kk=kk, hp=hp, w1_3=w1_3,
                        xn_3=xn_3: E.matmul(
                        slc(ps_h, hp, TC),
                        w1_3[:, 2 * kk:2 * kk + 2, m * 128:(m + 1) * 128],
                        xn_3[:, 2 * kk:2 * kk + 2, :],
                        start=(kk == 0), stop=(kk == 3),
                        perf_mode=mybir.MatmulPerfMode.DoubleRow),
                        st if kk == 3 else None)
                p.w("scalar", st)
                if m == 0 and c == 0:
                    p.w("scalar", dw, b_ready)
                if m >= 2:
                    p.w("scalar", sv, h8c[m - 2])
                p.o("scalar", lambda E, m=m, hp=hp: E.activation(
                    slc(h16b, hp, TC), slc(ps_h, hp, TC), AF.Relu,
                    bias=b1s[:, m:m + 1]), ss)
                h_evac[m] = ss.n
                p.w("vector", ss)
                p.o("vector", lambda E, m=m, hp=hp: E.tensor_copy(
                    slc(h8, m, TC), slc(h16b, hp, TC)), sv)
                h8c[m] = sv.n
            if tag == "P4" and "h8_dbg" in dt:
                p.w("gpsimd", sv, h8c[NH4 - 1])
                for m in range(NH4):
                    p.dma("gpsimd",
                          dt["h8_dbg"][m * 128:(m + 1) * 128,
                                       c * TC:(c + 1) * TC],
                          slc(h8, m, TC), dm)
                p.w("vector", dm)
            p.w("tensor", sv, h8c[NH4 - 1])
            y_evac = {}
            for nt in range(NDT):
                yp_ = nt % 2
                if nt >= 2:
                    p.w("tensor", ss, y_evac[nt - 2])
                w2_3 = w2t8.rearrange("p (k d) -> p k d", k=NH4)
                h8_3 = h8.rearrange("p (k t) -> p k t", k=NH4)
                for kk in range(NH4 // 2):
                    p.o("tensor", lambda E, nt=nt, kk=kk, yp_=yp_, w2_3=w2_3,
                        h8_3=h8_3: E.matmul(
                        slc(ps_y, yp_, TC),
                        w2_3[:, 2 * kk:2 * kk + 2, nt * 128:(nt + 1) * 128],
                        h8_3[:, 2 * kk:2 * kk + 2, :],
                        start=(kk == 0), stop=(kk == NH4 // 2 - 1),
                        perf_mode=mybir.MatmulPerfMode.DoubleRow),
                        st if kk == NH4 // 2 - 1 else None)
                p.w("scalar", st)
                if nt >= 2:
                    p.w("scalar", sv, drops_done[(c, nt - 2)])
                p.o("scalar", lambda E, nt=nt, yp_=yp_: E.activation(
                    slc(y16b, yp_, TC), slc(ps_y, yp_, TC), AF.Identity,
                    bias=b2s[:, nt:nt + 1]), ss)
                y_evac[nt] = ss.n
                p.w("vector", ss)
                p.o("vector", lambda E, nt=nt, yp_=yp_: E.tensor_mul(
                    dtmp[:, :], slc(y16b, yp_, TC), slc(dmb, nt, TC)))
                p.o("vector", lambda E, nt=nt, xoff=xoff: E.tensor_add(
                    slc(out_c, nt, TC),
                    x_ap[:, xoff + nt * TC: xoff + (nt + 1) * TC],
                    dtmp[:, :]), sv)
                drops_done[(c, nt)] = sv.n
            xread_done[c] = sv.n
            p.w("scalar", sv)
            outd[c] = p.dma(
                "scalar", out_r[:, :, c * TC:(c + 1) * TC],
                out_c.rearrange("p (a t) -> p a t", a=NDT), do)
            if xout_dst == "x1":
                deps["x1_c"][c] = outd[c]
        p.w("gpsimd", do)
        p.emit(nc)


def emit_attn(nc, dt, cons, sems, deps):
    sv, ss, st = sems["sv"], sems["ss"], sems["st"]
    dw, dx, do, dm = sems["dw"], sems["dx"], sems["do"], sems["dm"]
    with contextlib.ExitStack() as ph:
        p = Plan()
        wq = _sbp(nc, ph, "p5_wq", [128, 8 * D], BF16)
        wk = _sbp(nc, ph, "p5_wk", [128, 8 * D], BF16)
        wv = _sbp(nc, ph, "p5_wv", [128, 8 * D], BF16)
        wo = _sbp(nc, ph, "p5_wo", [128, 8 * D], BF16)
        bqs = _sbp(nc, ph, "p5_bq", [128, NDT], F32)
        bks = _sbp(nc, ph, "p5_bk", [128, NDT], F32)
        bos = _sbp(nc, ph, "p5_bo", [128, NDT], F32)
        bvf = _sbp(nc, ph, "p5_bvf", [1, D], F32)
        bvb = _sbp(nc, ph, "p5_bvb", [1, D], BF16)
        for geng in ("sync", "scalar", "gpsimd"):
            p.w(geng, do, deps["x1_c"][NCH - 1])
        p.dma("sync", bqs[:, :], dt["bq_t"][:, :], dw)
        p.dma("sync", bks[:, :], dt["bk_t"][:, :], dw)
        p.dma("sync", bos[:, :], dt["bo_t"][:, :], dw)
        p.dma("sync", bvf[:, :], dt["bv_row"][:, :], dw)
        b_ready = dw.n
        wcast = {}
        with contextlib.ExitStack() as prep:
            stage = _sbp(nc, prep, "p5_wst", [128, 2 * 2 * D], F32)
            for i, (wsb, wnm) in enumerate(
                    [(wq, "wq_t"), (wk, "wk_t"), (wv, "wv_t"), (wo, "wo_t")]):
                for g in range(4):
                    j = i * 4 + g
                    slot = j % 2
                    q, qs = (("sync", dw), ("scalar", do),
                             ("gpsimd", dm))[j % 3]
                    if j >= 2:
                        p.w(q, wcast[j - 2][0], wcast[j - 2][1])
                    p.dma(q, stage[:, slot * 2 * D:(slot + 1) * 2 * D]
                          .rearrange("p (k d) -> p k d", k=2),
                          dt[wnm][g * 256:(g + 1) * 256, :].rearrange(
                              "(k p) d -> p k d", p=128), qs)
                    ceng = "vector" if j % 2 == 0 else "scalar"
                    csem = sv if j % 2 == 0 else ss
                    p.w(ceng, qs)
                    if ceng == "vector":
                        p.o(ceng, lambda E, wsb=wsb, g=g, slot=slot:
                            E.tensor_copy(
                                wsb[:, g * 2 * D:(g + 1) * 2 * D],
                                stage[:, slot * 2 * D:(slot + 1) * 2 * D]),
                            csem)
                    else:
                        p.o(ceng, lambda E, wsb=wsb, g=g, slot=slot:
                            E.copy(
                                wsb[:, g * 2 * D:(g + 1) * 2 * D],
                                stage[:, slot * 2 * D:(slot + 1) * 2 * D]),
                            csem)
                    wcast[j] = (csem, csem.n)
        p.w("vector", dw, b_ready)
        p.o("vector", lambda E: E.tensor_copy(bvb[:, :], bvf[:, :]), sv)
        w_ready = sv.n
        w_ready_ss = ss.n
        x1b = _sbp(nc, ph, "p5_x1", [128, 2 * NDT * TC], F32)
        xnb = _sbp(nc, ph, "p5_xn", [128, NDT * TC], BF16)
        fq = _sbp(nc, ph, "p5_fq", [128, NDT * TC], BF16)
        fk = _sbp(nc, ph, "p5_fk", [128, NDT * TC], BF16)
        vtm = _sbp(nc, ph, "p5_vtm", [128, NSUB * D], BF16)
        ktm = _sbp(nc, ph, "p5_ktm", [128, NSUB * D], BF16)
        aos = _sbp(nc, ph, "p5_aos", [128, D], BF16)
        aoT = _sbp(nc, ph, "p5_aoT", [128, NDT * TC], BF16)
        out_c = _sbp(nc, ph, "p5_oc", [128, NDT * TC], F32)
        tq1 = _sbp(nc, ph, "p5_t1", [128, TC], F32)
        tq2 = _sbp(nc, ph, "p5_t2", [128, TC], F32)
        tq3 = _sbp(nc, ph, "p5_t3", [128, TC], F32)
        smb = _sbp(nc, ph, "p5_sm", [128, 2 * SUB], BF16)
        kvb = _sbp(nc, ph, "p5_kv", [128, NDT * DH], BF16)
        ksf = _sbp(nc, ph, "p5_ksf", [128, NDT], F32)
        kst = _sbp(nc, ph, "p5_kst", [128, NDT], F32)
        ksb = _sbp(nc, ph, "p5_ksb", [128, NDT], BF16)
        rall = _sbp(nc, ph, "p5_ra", [128, HEADS], F32)
        rinv = _sbp(nc, ph, "p5_ri", [128, HEADS], F32)
        lnt = _sbp(nc, ph, "p5_lnt", [128, TC], F32)
        lsr = _sbp(nc, ph, "p5_lsr", [1, 6 * TC], F32)
        ps_A = _psp(nc, ph, "p5_psA", [128, 2 * TC])
        ps_s0 = _psp(nc, ph, "p5_pss0", [128, SUB])
        ps_s1 = _psp(nc, ph, "p5_pss1", [128, SUB])
        ps_t0 = _psp(nc, ph, "p5_pst0", [128, SUB], BF16)
        ps_t1 = _psp(nc, ph, "p5_pst1", [128, SUB], BF16)
        ps_den = _psp(nc, ph, "p5_psd", [128, HEADS])
        ps_kv = _psp(nc, ph, "p5_pskv", [128, NDT * DH])
        lnb = {"xb": fk, "xsq": fq, "t": lnt, "srow": lsr,
               "sum": ps_A[0:1, 0:TC], "ssq": ps_A[0:1, TC:2 * TC],
               "bc": ps_A}

        p.o("vector", lambda E: E.memset(ksf[:, :], 0.0))
        p.o("vector", lambda E: E.memset(ksb[:, :], 0.0))
        p.o("vector", lambda E: E.memset(kvb[:, :], 0.0), sv)
        x1_r = dt["x1"].rearrange("(a p) t -> p a t", p=128)
        x2_r = dt["x2"].rearrange("(a p) t -> p a t", p=128)

        a_rd = w_ready
        a_rd_ss = 0
        a_wr = 0
        fk_rd = 0
        kv_ev = [0] * (NCH * NSUB + 1)
        xread = [0] * NCH
        outd = [0] * NCH
        first_kv = True
        for sc in range(NCH):
            slot = sc % 2
            if sc == 0:
                p.w("sync", sv, w_ready)
                p.w("sync", ss, w_ready_ss)
            if sc >= 2:
                p.w("sync", sv, xread[sc - 2])
            p.w("sync", do, deps["x1_c"][sc])
            p.dma("sync", slc(x1b, slot, NDT * TC).rearrange(
                "p (a t) -> p a t", a=NDT),
                x1_r[:, :, sc * TC:(sc + 1) * TC], dx)
            p.w("vector", dx)
            p.w("vector", st, fk_rd)
            if sc == 0:
                p.w("vector", sv, w_ready)
                p.w("scalar", dw, b_ready)
                p.w("scalar", ss, w_ready_ss)
            xoff = slot * NDT * TC
            xn_done = emit_ln(
                p, sems, x1b[:, xoff:xoff + NDT * TC],
                lambda a: slc(xnb, a, TC), TC, cons, lnb,
                psum_guard=[(sv, a_rd), (ss, a_rd_ss)], x_gate=(dx, dx.n))
            a_wr = st.n
            a_rd = xn_done
            pe_gate = max(xn_done, w_ready)
            p.w("tensor", sv, pe_gate)
            if sc == 0:
                p.w("tensor", ss, w_ready_ss)
            prev_ev = {}
            prev_relu = {}
            for i in range(2 * NDT):
                m = i % NDT
                wsb, bias, dst = (wq, bqs, fq) if i < NDT else (wk, bks, fk)
                pp = i % 2
                if i >= 2:
                    p.w("tensor", sv, prev_ev[i - 2])
                    p.w("tensor", ss, prev_relu[i - 2])
                for k in range(NDT):
                    p.o("tensor", lambda E, wsb=wsb, m=m, k=k, pp=pp: E.matmul(
                        slc(ps_A, pp, TC),
                        wsb[:, k * D + m * 128: k * D + (m + 1) * 128],
                        slc(xnb, k, TC), start=(k == 0), stop=(k == NDT - 1)),
                        st if k == NDT - 1 else None)
                p.w("scalar", st)
                if i >= 1:
                    p.w("scalar", sv, prev_ev[i - 1])
                p.o("scalar", lambda E, bias=bias, m=m, pp=pp: E.activation(
                    tq1[:, :], slc(ps_A, pp, TC), AF.Relu,
                    bias=bias[:, m:m + 1]), ss)
                prev_relu[i] = ss.n
                p.w("vector", st)
                p.w("vector", ss)
                p.o("vector", lambda E, bias=bias, m=m, pp=pp: E.tensor_scalar(
                    tq2[:, :], slc(ps_A, pp, TC), bias[:, m:m + 1], 0.0,
                    ALU.add, ALU.min), sv)
                p.w("scalar", sv)
                p.o("scalar", lambda E: E.activation(
                    tq3[:, :], tq2[:, :], AF.Exp), ss)
                p.w("vector", ss)
                p.o("vector", lambda E, dst=dst, m=m: E.tensor_add(
                    slc(dst, m, TC), tq1[:, :], tq3[:, :]), sv)
                prev_ev[i] = sv.n
            qk_done = sv.n
            a_wr = st.n
            a_rd = qk_done
            a_rd_ss = ss.n
            if "fq_dbg" in dt:
                p.w("gpsimd", sv, qk_done)
                for m in range(NDT):
                    p.dma("gpsimd", dt["fq_dbg"][m * 128:(m + 1) * 128,
                                                 sc * TC:(sc + 1) * TC],
                          slc(fq, m, TC), dm)
                    p.dma("gpsimd", dt["fk_dbg"][m * 128:(m + 1) * 128,
                                                 sc * TC:(sc + 1) * TC],
                          slc(fk, m, TC), dm)
            vt_ev = {}
            for s in range(NSUB):
                for nh in range(2):
                    j = s * 2 + nh
                    pp = j % 2
                    if j >= 2:
                        p.w("tensor", sv, vt_ev[j - 2])
                    else:
                        p.w("tensor", sv, prev_ev[2 * NDT - 2 + j])
                        p.w("tensor", ss, prev_relu[2 * NDT - 2 + j])
                    for k in range(NDT):
                        p.o("tensor", lambda E, s=s, nh=nh, k=k, pp=pp:
                            E.matmul(
                                slc(ps_A, pp, TC),
                                xnb[:, k * TC + s * SUB: k * TC + (s + 1) * SUB],
                                wv[:, k * D + nh * TC: k * D + (nh + 1) * TC],
                                start=(k == 0), stop=False))
                    p.o("tensor", lambda E, nh=nh, pp=pp: E.matmul(
                        slc(ps_A, pp, TC), cons["ones_row_bf"][:, :],
                        bvb[0:1, nh * TC:(nh + 1) * TC],
                        start=False, stop=True), st)
                    p.w("vector", st)
                    p.o("vector", lambda E, s=s, nh=nh, pp=pp: E.tensor_copy(
                        vtm[:, s * D + nh * TC: s * D + (nh + 1) * TC],
                        slc(ps_A, pp, TC)), sv)
                    vt_ev[j] = sv.n
            a_wr = st.n
            a_rd = sv.n
            if "vtm_dbg" in dt:
                p.w("gpsimd", sv)
                for s_ in range(NSUB):
                    p.dma("gpsimd",
                          dt["vtm_dbg"][sc * TC + s_ * SUB:
                                        sc * TC + (s_ + 1) * SUB, :],
                          vtm[:, s_ * D:(s_ + 1) * D], dm)
            p.w("tensor", sv, qk_done)
            tr_ev = {}
            for j in range(NDT * NSUB):
                m, s = j % NDT, j // NDT
                pst = ps_t0 if j % 2 == 0 else ps_t1
                if j >= 2:
                    p.w("tensor", sv, tr_ev[j - 2])
                p.o("tensor", lambda E, m=m, s=s, pst=pst: E.transpose(
                    pst[:, :], fk[:, m * TC + s * SUB: m * TC + (s + 1) * SUB],
                    cons["ident_bf"][:, :]), st)
                p.w("vector", st)
                p.o("vector", lambda E, m=m, s=s, pst=pst: E.tensor_copy(
                    ktm[:, s * D + m * 128: s * D + (m + 1) * 128],
                    pst[:, :]), sv)
                tr_ev[j] = sv.n
            ktm_done = sv.n
            for s in range(NSUB):
                gsub = sc * NSUB + s
                p.w("tensor", sv, max(ktm_done, kv_ev[gsub], a_rd))
                sm_done = {}
                num_done = {}
                s_mm = {}

                def head_slices(h):
                    fkh = fk[64 * (h % 2):64 * (h % 2) + 64,
                             (h // 2) * TC + s * SUB:
                             (h // 2) * TC + (s + 1) * SUB]
                    fqh = fq[64 * (h % 2):64 * (h % 2) + 64,
                             (h // 2) * TC + s * SUB:
                             (h // 2) * TC + (s + 1) * SUB]
                    return fkh, fqh

                def emit_sprime(h):
                    fkh, fqh = head_slices(h)
                    pss = ps_s0 if h % 2 == 0 else ps_s1
                    if h >= 2:
                        p.w("tensor", sv, sm_done[h - 2])
                    p.o("tensor", lambda E, fkh=fkh, fqh=fqh, pss=pss:
                        E.matmul(pss[:, :], fkh, fqh, start=True, stop=True),
                        st)
                    s_mm[h] = st.n

                emit_sprime(0)
                for h in range(HEADS):
                    hp = h % 2
                    fkh, fqh = head_slices(h)
                    pss = ps_s0 if h % 2 == 0 else ps_s1
                    p.w("vector", st, s_mm[h])
                    if h >= 2:
                        p.w("vector", st, num_done[h - 2])
                    p.o("vector", lambda E, hp=hp, pss=pss: E.tensor_mul(
                        slc(smb, hp, SUB), pss[:, :], cons["cmask_bf"][:, :]),
                        sv)
                    sm_done[h] = sv.n
                    if h + 1 < HEADS:
                        emit_sprime(h + 1)
                    p.w("tensor", sv, sm_done[h])
                    p.o("tensor", lambda E, hp=hp, h=h, s=s: E.matmul(
                        ps_den[:, h:h + 1], slc(smb, hp, SUB),
                        cons["ones_col_bf"][:, :], start=True, stop=False))
                    p.o("tensor", lambda E, fqh=fqh, h=h: E.matmul(
                        ps_den[:, h:h + 1], fqh,
                        ksb[64 * (h % 2):64 * (h % 2) + 64, h // 2:h // 2 + 1],
                        start=False, stop=True))
                    p.o("tensor", lambda E, hp=hp, h=h, s=s: E.matmul(
                        ps_A[:, h * DH:(h + 1) * DH], slc(smb, hp, SUB),
                        vtm[:, s * D + h * DH: s * D + (h + 1) * DH],
                        start=True, stop=False))
                    p.o("tensor", lambda E, fqh=fqh, h=h: E.matmul(
                        ps_A[:, h * DH:(h + 1) * DH], fqh,
                        kvb[64 * (h % 2):64 * (h % 2) + 64,
                            (h // 2) * DH:(h // 2 + 1) * DH],
                        start=False, stop=True), st)
                    num_done[h] = st.n
                heads_done = st.n
                fk_rd = heads_done
                for h in range(HEADS):
                    p.o("tensor", lambda E, h=h, s=s: E.matmul(
                        ps_kv[64 * (h % 2):64 * (h % 2) + 64,
                              (h // 2) * DH:(h // 2 + 1) * DH],
                        ktm[:, s * D + h * DH: s * D + (h + 1) * DH],
                        vtm[:, s * D + h * DH: s * D + (h + 1) * DH],
                        start=first_kv, stop=True, skip_group_check=True,
                        tile_position=(0, 64 * (h % 2))),
                        st if h == HEADS - 1 else None)
                first_kv = False
                kvmm_done = st.n
                p.w("vector", st, heads_done)
                for m in range(NDT):
                    p.o("vector", lambda E, m=m, s=s: E.tensor_reduce(
                        kst[:, m:m + 1],
                        fk[:, m * TC + s * SUB: m * TC + (s + 1) * SUB],
                        AX.X, ALU.add))
                p.o("vector", lambda E: E.tensor_single_scalar(
                    rall[:, :], ps_den[:, :], 1e-6, ALU.max))
                p.o("vector", lambda E: E.tensor_add(
                    ksb[:, :], ksf[:, :], kst[:, :]))
                p.o("vector", lambda E: E.tensor_add(
                    ksf[:, :], ksf[:, :], kst[:, :]))
                p.o("vector", lambda E: E.reciprocal(rinv[:, :], rall[:, :]))
                p.w("vector", st, kvmm_done)
                p.o("vector", lambda E: E.tensor_copy(kvb[:, :], ps_kv[:, :]),
                    sv)
                kv_ev[gsub + 1] = sv.n
                if "den_dbg" in dt:
                    p.w("gpsimd", sv, kv_ev[gsub + 1])
                    dden = p.dma("gpsimd",
                                 dt["den_dbg"][gsub * SUB:(gsub + 1) * SUB, :],
                                 rall[:, :], dm)
                for h in range(HEADS):
                    p.o("vector", lambda E, h=h: E.tensor_scalar_mul(
                        aos[:, h * DH:(h + 1) * DH],
                        ps_A[:, h * DH:(h + 1) * DH], rinv[:, h:h + 1]),
                        sv if h == HEADS - 1 else None)
                a_rd = sv.n
                aos_done = sv.n
                if "aos_dbg" in dt:
                    p.w("gpsimd", sv, aos_done)
                    aosd = p.dma("gpsimd",
                                 dt["aos_dbg"][gsub * SUB:(gsub + 1) * SUB, :],
                                 aos[:, :], dm)
                    p.w("vector", dm, aosd)
                p.w("tensor", sv, aos_done)
                for m in range(NDT):
                    pst = ps_t0 if m % 2 == 0 else ps_t1
                    if m >= 2:
                        p.w("tensor", sv, tr_ev[NDT * NSUB + m - 2])
                    p.o("tensor", lambda E, m=m, pst=pst: E.transpose(
                        pst[:, :], aos[:, m * 128:(m + 1) * 128],
                        cons["ident_bf"][:, :]), st)
                    p.w("vector", st)
                    p.o("vector", lambda E, m=m, s=s, pst=pst: E.tensor_copy(
                        aoT[:, m * TC + s * SUB: m * TC + (s + 1) * SUB],
                        pst[:, :]), sv)
                    tr_ev[NDT * NSUB + m] = sv.n
                aot_done = sv.n
            if "aot_dbg" in dt:
                p.w("gpsimd", sv, aot_done)
                ad = p.dma("gpsimd",
                           dt["aot_dbg"][:, sc * NDT * TC:(sc + 1) * NDT * TC],
                           aoT[:, :], dm)
                if sc == 0 and "wo_dbg" in dt:
                    p.dma("gpsimd", dt["wo_dbg"][:, :], wo[:, :], dm)
            p.w("tensor", sv, aot_done)
            wo_ev = {}
            wo_sv = {}
            for nt in range(NDT):
                pp = nt % 2
                if nt >= 2:
                    p.w("tensor", sv, wo_ev[nt - 2])
                    p.w("tensor", ss, wo_sv[nt - 2])
                for k in range(NDT):
                    p.o("tensor", lambda E, nt=nt, k=k, pp=pp: E.matmul(
                        slc(ps_A, pp, TC),
                        wo[:, k * D + nt * 128: k * D + (nt + 1) * 128],
                        slc(aoT, k, TC), start=(k == 0), stop=(k == NDT - 1)),
                        st if k == NDT - 1 else None)
                p.w("scalar", st)
                if nt >= 1:
                    p.w("scalar", sv, wo_ev[nt - 1])
                p.o("scalar", lambda E, nt=nt, pp=pp: E.activation(
                    tq1[:, :], slc(ps_A, pp, TC), AF.Identity,
                    bias=bos[:, nt:nt + 1]), ss)
                wo_sv[nt] = ss.n
                p.w("vector", ss)
                if sc >= 1 and nt == 0:
                    p.w("vector", do, outd[sc - 1])
                p.o("vector", lambda E, nt=nt, xoff=xoff: E.tensor_add(
                    slc(out_c, nt, TC),
                    x1b[:, xoff + nt * TC: xoff + (nt + 1) * TC],
                    tq1[:, :]), sv)
                wo_ev[nt] = sv.n
            a_wr = st.n
            a_rd = sv.n
            a_rd_ss = ss.n
            xread[sc] = sv.n
            p.w("scalar", sv)
            outd[sc] = p.dma(
                "scalar", x2_r[:, :, sc * TC:(sc + 1) * TC],
                out_c.rearrange("p (a t) -> p a t", a=NDT), do)
            deps["x2_c"][sc] = outd[sc]
        p.w("gpsimd", do)
        p.emit(nc)


def host_inputs(inputs, core):
    f32 = np.float32

    def t2(a):
        n = a.shape[0] // 128
        return np.ascontiguousarray(
            a.reshape(n, 128, -1).transpose(1, 0, 2).reshape(128, -1))

    x = inputs["L0_attn_a_x"][core]
    d = {
        "xin": np.ascontiguousarray(x.T).astype(f32),
        "w_in_blk": np.ascontiguousarray(
            inputs["L0_attn_a_W_in"].reshape(8, 128, 32, 128)
            .transpose(2, 1, 0, 3).reshape(32 * 128, 8 * 128)),
        "w_delta_blk": np.ascontiguousarray(
            inputs["L0_attn_a_W_delta"].reshape(16, 128, 16, 128)
            .transpose(2, 1, 0, 3).reshape(16 * 128, 16 * 128)),
        "w_out_blk": np.ascontiguousarray(
            inputs["L0_attn_a_W_out"].reshape(16, 128, 8, 128)
            .transpose(2, 1, 0, 3).reshape(8 * 128, 16 * 128)),
        "w_b_t": t2(inputs["L0_attn_a_W_B"]),
        "w_c_t": t2(inputs["L0_attn_a_W_C"]),
        "a_log_t": t2(inputs["L0_attn_a_A_log"]),
        "b_delta_t": np.ascontiguousarray(
            inputs["L0_attn_a_b_delta"].reshape(16, 128).T),
        "d_param_t": np.ascontiguousarray(
            inputs["L0_attn_a_D_param"].reshape(16, 128).T),
        "w1_t": np.ascontiguousarray(inputs["L0_ffn_w1"].T),
        "w2_t": np.ascontiguousarray(inputs["L0_ffn_w2"].T),
        "b1_t": np.ascontiguousarray(inputs["L0_ffn_b1"].reshape(32, 128).T),
        "b2_t": np.ascontiguousarray(inputs["L0_ffn_b2"].reshape(8, 128).T),
        "wq_t": np.ascontiguousarray(inputs["L0_attn_b_Wq"].T),
        "wk_t": np.ascontiguousarray(inputs["L0_attn_b_Wk"].T),
        "wv_t": np.ascontiguousarray(inputs["L0_attn_b_Wv"].T),
        "wo_t": np.ascontiguousarray(inputs["L0_attn_b_Wo"].T),
        "bq_t": np.ascontiguousarray(inputs["L0_attn_b_bq"].reshape(8, 128).T),
        "bk_t": np.ascontiguousarray(inputs["L0_attn_b_bk"].reshape(8, 128).T),
        "bv_row": np.ascontiguousarray(inputs["L0_attn_b_bv"].reshape(1, D)),
        "bo_t": np.ascontiguousarray(inputs["L0_attn_b_bo"].reshape(8, 128).T),
        "dropm": _dropmask_bf16()[core],
    }
    return d


_DROPM = None


def _threefry2x32(k0, k1, x0, x1):
    rot = (13, 15, 26, 6, 17, 29, 16, 24)
    x0 = x0.astype(np.uint32).copy()
    x1 = x1.astype(np.uint32).copy()
    ks = (np.uint32(k0), np.uint32(k1),
          np.uint32(k0 ^ k1 ^ np.uint32(0x1BD11BDA)))
    x0 += ks[0]
    x1 += ks[1]
    for i in range(5):
        for r in (rot[:4] if i % 2 == 0 else rot[4:]):
            x0 += x1
            x1 = ((x1 << np.uint32(r)) | (x1 >> np.uint32(32 - r))).astype(
                np.uint32)
            x1 ^= x0
        x0 += ks[(i + 1) % 3]
        x1 += ks[(i + 2) % 3] + np.uint32(i + 1)
    return x0, x1


def _dropmask_bf16():
    global _DROPM
    if _DROPM is None:
        n = 8 * T * D
        io = np.arange(n, dtype=np.uint64)
        c1 = (io >> np.uint64(32)).astype(np.uint32)
        c2 = (io & np.uint64(0xFFFFFFFF)).astype(np.uint32)
        b1, b2 = _threefry2x32(0, 42, c1, c2)
        bits = b1 ^ b2
        u = ((bits >> np.uint32(9)) | np.uint32(0x3F800000)).view(
            np.float32) - 1.0
        keep = (u < np.float32(0.8)).reshape(8, T, D)
        m = (keep.astype(np.float32) * 1.25).astype(BF)
        _DROPM = [np.ascontiguousarray(m[c].T) for c in range(8)]
    return _DROPM


def kernel(**inputs):
    nc, _ = build_program()
    in_maps = [host_inputs(inputs, core) for core in range(8)]
    res = run_bass_kernel_spmd(nc, in_maps, core_ids=list(range(8)))
    outs = [np.ascontiguousarray(r["out"].T) for r in res.results]
    return np.stack(outs, axis=0).astype(np.float32)


# version 25
# speedup vs baseline: 1.0123x; 1.0123x over previous
import contextlib
import numpy as np
import ml_dtypes

import concourse.bass as bass
import concourse.mybir as mybir
from concourse.bass_utils import run_bass_kernel_spmd

F32 = mybir.dt.float32
F16 = mybir.dt.float16
BF16 = mybir.dt.bfloat16
FP8 = mybir.dt.float8e4
ALU = mybir.AluOpType
AF = mybir.ActivationFunctionType
AX = mybir.AxisListType

D = 1024
DI = 2048
H4 = 4096
NST = 16
T = 2048
TC = 512
NCH = T // TC
NDT = D // 128
NDI = DI // 128
NH4 = H4 // 128
HEADS = 16
DH = 64
SUB = 128
NSUB = TC // SUB
EPS_LN = 1e-5

ENGINES = ("sync", "tensor", "vector", "scalar", "gpsimd")
BF = ml_dtypes.bfloat16


class Ctr:
    __slots__ = ("h", "n", "name")

    def __init__(self, h, name):
        self.h = h
        self.n = 0
        self.name = name


class Plan:
    def __init__(self):
        self.st = {e: [] for e in ENGINES}

    def w(self, eng, ctr, val=None):
        v = ctr.n if val is None else val
        assert v <= ctr.n, f"wait on future {ctr.name}: {v} > {ctr.n}"
        if v <= 0:
            return
        self.st[eng].append(lambda E, h=ctr.h, v=v: E.wait_ge(h, v))

    def o(self, eng, fn, inc=None, k=1):
        if inc is None:
            self.st[eng].append(fn)
            return None
        inc.n += k
        self.st[eng].append(lambda E, f=fn, h=inc.h, kk=k: f(E).then_inc(h, kk))
        return inc.n

    def dma(self, eng, out, in_, inc):
        return self.o(eng, lambda E, o=out, i=in_: E.dma_start(out=o, in_=i),
                      inc, 16)

    def emit(self, nc):
        with nc.Block() as blk:
            for eng in ENGINES:
                ops = self.st[eng]
                if not ops:
                    continue
                getattr(blk, eng)(lambda E, ops=ops: [f(E) for f in ops])


def slc(ap, i, w):
    return ap[:, i * w:(i + 1) * w]


def build_program(phases=("P1", "P2", "P4", "P5", "P6"), debug_out=()):
    nc = bass.Bass("TRN2", target_bir_lowering=False)
    stack = contextlib.ExitStack()
    dt = {}

    def din(name, shape, dtype=F32):
        dt[name] = nc.dram_tensor(name, shape, dtype, kind="ExternalInput").ap()

    din("xin", [D, T])
    din("w_in_blk", [NH4 * 128, 8 * 128])
    din("w_delta_blk", [NDI * 128, NDI * 128])
    din("w_out_blk", [NDT * 128, NDI * 128])
    din("w_b_t", [128, NDI * NST])
    din("w_c_t", [128, NDI * NST])
    din("a_log_t", [128, NDI * NST])
    din("b_delta_t", [128, NDI])
    din("d_param_t", [128, NDI])
    din("w1_t", [D, H4])
    din("w2_t", [H4, D])
    din("b1_t", [128, NH4])
    din("b2_t", [128, NDT])
    din("wq_t", [D, D])
    din("wk_t", [D, D])
    din("wv_t", [D, D])
    din("wo_t", [D, D])
    din("bq_t", [128, NDT])
    din("bk_t", [128, NDT])
    din("bv_row", [1, D])
    din("bo_t", [128, NDT])
    din("dropm", [D, T], BF16)

    out_names = set(debug_out)

    def scratch(name, shape, dtype):
        kind = "ExternalOutput" if name in out_names else "Internal"
        dt[name] = nc.dram_tensor(name, shape, dtype, kind=kind).ap()

    scratch("sres", [DI, T], BF16)
    scratch("mout", [D, T], BF16)
    scratch("x1", [D, T], F32)
    scratch("x2", [D, T], F32)
    if "u_dbg" in out_names:
        scratch("u_dbg", [DI, T], BF16)
    if "del_dbg" in out_names:
        scratch("del_dbg", [DI, T], F16)
    if "xn8_dbg" in out_names:
        scratch("xn8_dbg", [D, T], FP8)
    if "h8_dbg" in out_names:
        scratch("h8_dbg", [H4, T], FP8)
    if "fq_dbg" in out_names:
        scratch("fq_dbg", [D, T], BF16)
        scratch("fk_dbg", [D, T], BF16)
    if "vtm_dbg" in out_names:
        scratch("vtm_dbg", [T, D], BF16)
    if "den_dbg" in out_names:
        scratch("den_dbg", [T, HEADS], F32)
    if "aos_dbg" in out_names:
        scratch("aos_dbg", [T, D], BF16)
    if "aot_dbg" in out_names:
        scratch("aot_dbg", [128, NCH * NDT * TC], BF16)
    if "wo_dbg" in out_names:
        scratch("wo_dbg", [128, 8 * D], BF16)
    if "my_dbg" in out_names:
        scratch("my_dbg", [DI, T], BF16)
    if "bc_dbg" in out_names:
        scratch("bc_dbg", [32, T], BF16)
    dt["out"] = nc.dram_tensor("out", [D, T], F32, kind="ExternalOutput").ap()

    sel = np.zeros((16, NST * 128), np.float32)
    for n in range(NST):
        sel[n, n * 128:(n + 1) * 128] = 1.0
    const_np = {
        "cvals": np.broadcast_to(
            np.array([EPS_LN, 1.0], np.float32), (128, 2)).copy(),
        "ones_col_bf": np.ones((128, 1), np.float32).astype(BF),
        "ones_row_f32": np.ones((1, 128), np.float32),
        "ones_row_bf": np.ones((1, 128), np.float32).astype(BF),
        "ident_bf": np.eye(128, dtype=np.float32).astype(BF),
        "cmask_bf": np.triu(np.ones((128, 128), np.float32)).astype(BF),
        "sel_bf": sel.astype(BF),
    }
    const_dt = {"float32": F32, "bfloat16": BF16}

    sems = {}
    for s in ("sv", "ss", "st", "sg", "dw", "dx", "do", "dm"):
        sems[s] = Ctr(stack.enter_context(nc.semaphore(s)), s)
    sv, ss, st, sg = sems["sv"], sems["ss"], sems["st"], sems["sg"]
    dw, dx, do, dm = sems["dw"], sems["dx"], sems["do"], sems["dm"]

    cons = {}
    p0 = Plan()
    for nm, arr in const_np.items():
        dram = nc.inline_tensor(arr, name="ci_" + nm).ap()
        sb = _sbp(nc, stack, "c_" + nm, list(arr.shape),
                  const_dt[str(arr.dtype)])
        cons[nm] = sb
        p0.dma("sync", sb[:, :], dram[:, :], dw)
    for eng in ENGINES:
        p0.w(eng, dw)
    p0.emit(nc)

    deps = {"mout_c": [0] * NCH, "x1_c": [0] * NCH, "x2_c": [0] * NCH}
    mstack = contextlib.ExitStack()
    u = _sbp(nc, mstack, "u", [128, NDI * T], BF16)

    def uslc(ki, c):
        return u[:, ki * T + c * TC: ki * T + (c + 1) * TC]

    if "P1" in phases:
        with contextlib.ExitStack() as ph:
            p = Plan()
            xn = _sbp(nc, ph, "p1_xn", [128, NDT * T], BF16)
            xin_b = _sbp(nc, ph, "p1_xin", [128, 2 * NDT * TC], F32)
            wstage = _sbp(nc, ph, "p1_wst", [128, 2 * 1024], F32)
            wbf = _sbp(nc, ph, "p1_wbf", [128, 2 * 1024], BF16)
            evb = _sbp(nc, ph, "p1_ev", [128, 2 * TC], BF16)
            lnb = alloc_ln_bufs(nc, ph, TC, "p1")
            ps_u = _psp(nc, ph, "p1_psu", [128, 2 * TC])

            xin_r = dt["xin"].rearrange("(a p) t -> p a t", p=128)
            xn_done = [0] * NCH
            for c in range(NCH):
                buf = slc(xin_b, c % 2, NDT * TC)
                if c >= 2:
                    p.w("sync", sv, xn_done[c - 2])
                p.dma("sync", buf.rearrange("p (a t) -> p a t", a=NDT),
                      xin_r[:, :, c * TC:(c + 1) * TC], dx)
                p.w("vector", dx)
                p.w("scalar", dx)
                xn_done[c] = emit_ln(
                    p, sems, buf,
                    lambda a, c=c: xn[:, a * T + c * TC: a * T + (c + 1) * TC],
                    TC, cons, lnb)
            cast_done = [0] * NH4
            mm_done = [0] * NH4
            evac_done = {}
            res_dma = {}
            p.w("tensor", sv, xn_done[NCH - 1])
            for m in range(NH4):
                par = m % 2
                if m >= 2:
                    p.w("sync", ss, cast_done[m - 2])
                p.dma("sync", slc(wstage, par, 1024),
                      dt["w_in_blk"][m * 128:(m + 1) * 128, :], dw)
                p.w("scalar", dw)
                if m >= 2:
                    p.w("scalar", st, mm_done[m - 2])
                p.o("scalar", lambda E, par=par: E.copy(
                    slc(wbf, par, 1024), slc(wstage, par, 1024)), ss)
                cast_done[m] = ss.n
                p.w("tensor", ss, cast_done[m])
                for c in range(NCH):
                    idx = m * NCH + c
                    pp = idx % 2
                    if idx >= 2:
                        p.w("tensor", ss, evac_done[idx - 2])
                    for k in range(8):
                        p.o("tensor", lambda E, par=par, k=k, c=c, pp=pp:
                            E.matmul(slc(ps_u, pp, TC),
                                     wbf[:, par * 1024 + k * 128:
                                         par * 1024 + (k + 1) * 128],
                                     xn[:, k * T + c * TC: k * T + (c + 1) * TC],
                                     start=(k == 0), stop=(k == 7)),
                            st if k == 7 else None)
                    mm_done[m] = st.n
                    p.w("scalar", st, mm_done[m])
                    if m < NDI:
                        p.o("scalar", lambda E, m=m, c=c, pp=pp: E.activation(
                            uslc(m, c), slc(ps_u, pp, TC), AF.Silu), ss)
                        evac_done[idx] = ss.n
                    else:
                        ridx = (m - NDI) * NCH + c
                        rp = ridx % 2
                        if ridx >= 2:
                            p.w("scalar", do, res_dma[ridx - 2])
                        p.o("scalar", lambda E, rp=rp, pp=pp: E.activation(
                            slc(evb, rp, TC), slc(ps_u, pp, TC), AF.Silu), ss)
                        evac_done[idx] = ss.n
                        res_dma[ridx] = p.dma(
                            "scalar",
                            dt["sres"][(m - NDI) * 128:(m - NDI + 1) * 128,
                                       c * TC:(c + 1) * TC],
                            slc(evb, rp, TC), do)
            if "u_dbg" in out_names:
                p.w("gpsimd", ss)
                for ki in range(NDI):
                    p.dma("gpsimd", dt["u_dbg"][ki * 128:(ki + 1) * 128, :],
                          u[:, ki * T:(ki + 1) * T], dm)
                p.w("gpsimd", dm)
            p.emit(nc)
    u_ready = ss.n

    if "P2" in phases:
        with contextlib.ExitStack() as ph:
            p = Plan()
            kw_st = _sbp(nc, ph, "p2_kwst", [128, 3 * 2048], F32)
            kw_bf = _sbp(nc, ph, "p2_kwbf", [128, 3 * 2048], BF16)
            wb_sb = _sbp(nc, ph, "p2_wb", [128, NDI * NST], BF16)
            wc_sb = _sbp(nc, ph, "p2_wc", [128, NDI * NST], BF16)
            a_sb = _sbp(nc, ph, "p2_a", [128, NDI * NST], F32)
            bd_sb = _sbp(nc, ph, "p2_bd", [128, NDI], F32)
            dp_sb = _sbp(nc, ph, "p2_dp", [128, NDI], F32)
            B_sb = _sbp(nc, ph, "p2_B", [16, TC], BF16)
            C_sb = _sbp(nc, ph, "p2_C", [16, TC], BF16)
            Bbc = _sbp(nc, ph, "p2_Bbc", [128, NST * TC], BF16)
            Cbc = _sbp(nc, ph, "p2_Cbc", [128, NST * TC], BF16)
            d16 = _sbp(nc, ph, "p2_d16", [128, TC], F16)
            dub = _sbp(nc, ph, "p2_du", [128, TC], BF16)
            sp_e = _sbp(nc, ph, "p2_spe", [128, TC], F32)
            sp_t1 = _sbp(nc, ph, "p2_spt1", [128, TC], F32)
            sp_t2 = _sbp(nc, ph, "p2_spt2", [128, TC], F32)
            dab = _sbp(nc, ph, "p2_dab", [128, 4 * TC], F32)
            bb = _sbp(nc, ph, "p2_bb", [128, 4 * TC], BF16)
            hs = _sbp(nc, ph, "p2_hs", [128, 4 * TC], BF16)
            hcb = _sbp(nc, ph, "p2_hc", [128, 4 * TC], BF16)
            yp = _sbp(nc, ph, "p2_yp", [128, TC], F32)
            yp12 = _sbp(nc, ph, "p2_yp12", [128, 2 * TC], F32)
            yd = _sbp(nc, ph, "p2_yd", [128, TC], F32)
            hstate = _sbp(nc, ph, "p2_hst", [128, NDI * NST], F32)
            myb = _sbp(nc, ph, "p2_my", [128, NDI * TC], BF16)
            ost = _sbp(nc, ph, "p2_ost", [128, 2 * TC], BF16)
            srb = _sbp(nc, ph, "p2_srb", [128, 2 * TC], BF16)
            ps_d = _psp(nc, ph, "p2_psd", [128, 2 * TC])
            ps_sb = _psp(nc, ph, "p2_pssb", [16, TC])
            ps_sc = _psp(nc, ph, "p2_pssc", [16, TC])
            ps_br = _psp(nc, ph, "p2_psbr", [128, 2 * TC])
            ps_o = _psp(nc, ph, "p2_pso", [128, 2 * TC])

            p.w("sync", ss, u_ready)
            p.dma("sync", kw_st[:, 0:256], dt["w_b_t"][:, :], dw)
            p.dma("sync", kw_st[:, 256:512], dt["w_c_t"][:, :], dw)
            p.dma("sync", kw_st[:, 512:768], dt["a_log_t"][:, :], dw)
            p.dma("sync", bd_sb[:, :], dt["b_delta_t"][:, :], dw)
            p.dma("sync", dp_sb[:, :], dt["d_param_t"][:, :], dw)
            p.w("vector", dw)
            p.o("vector", lambda E: E.tensor_copy(wb_sb[:, :], kw_st[:, 0:256]))
            p.o("vector", lambda E: E.tensor_copy(wc_sb[:, :], kw_st[:, 256:512]))
            p.o("vector", lambda E: E.memset(hstate[:, :], 0.0), sv)
            p.w("scalar", dw)
            p.o("scalar", lambda E: E.activation(
                sp_e[:, 0:256], kw_st[:, 512:768], AF.Exp), ss)
            p.w("vector", ss)
            p.o("vector", lambda E: E.tensor_scalar_mul(
                a_sb[:, :], sp_e[:, 0:256], -1.0), sv)
            prep_done_v = sv.n
            prep_done_s = ss.n
            p.w("tensor", sv, prep_done_v)
            p.w("tensor", ss, u_ready)

            kw_cast = {}
            kw_mmdone = {}
            bc_evac_c = [0] * NCH
            my_done_c = [0] * NCH
            wout_done_c = [0] * NCH
            bsb_evac = [0] * NCH
            ost_dma = {}
            out_evac = {}
            sres_dma = {}
            psd_free = {}
            dbg_del = {}
            scan_hist = []

            def kw_block(bi, src_rows):
                par = bi % 3
                if bi >= 3:
                    p.w("sync", ss, kw_cast[bi - 3])
                else:
                    p.w("sync", sv, prep_done_v)
                    p.w("sync", ss, prep_done_s)
                p.dma("sync", slc(kw_st, par, 2048), src_rows, dw)
                p.w("scalar", dw)
                if bi >= 3:
                    p.w("scalar", st, kw_mmdone[bi - 3])
                p.o("scalar", lambda E, par=par: E.copy(
                    slc(kw_bf, par, 2048), slc(kw_st, par, 2048)), ss)
                kw_cast[bi] = ss.n

            def sres_load(dc, c):
                i = c * NDI + dc
                if i >= 2:
                    p.w("scalar", sv, my_dc_done.get(i - 2, 0))
                sres_dma[i] = p.dma(
                    "scalar", slc(srb, dc % 2, TC),
                    dt["sres"][dc * 128:(dc + 1) * 128, c * TC:(c + 1) * TC], dm)

            my_dc_done = {}

            def blk_src(bj):
                cj, rj = divmod(bj, NDI + NDT)
                if rj < NDI:
                    return dt["w_delta_blk"][rj * 128:(rj + 1) * 128, :]
                return dt["w_out_blk"][(rj - NDI) * 128:(rj - NDI + 1) * 128, :]

            def emit_wout(c):
                p.w("tensor", sv, my_done_c[c])
                for mt in range(NDT):
                    bi = c * (NDI + NDT) + NDI + mt
                    if bi + 1 < NCH * (NDI + NDT):
                        kw_block(bi + 1, blk_src(bi + 1))
                    opar = mt % 2
                    p.w("tensor", ss, kw_cast[bi])
                    oi = c * NDT + mt
                    if oi >= 2:
                        p.w("tensor", ss, out_evac[oi - 2])
                    for k in range(NDI):
                        p.o("tensor", lambda E, bip=bi % 3, k=k, opar=opar:
                            E.matmul(slc(ps_o, opar, TC),
                                     kw_bf[:, bip * 2048 + k * 128:
                                           bip * 2048 + (k + 1) * 128],
                                     slc(myb, k, TC), start=(k == 0),
                                     stop=(k == NDI - 1)),
                            st if k == NDI - 1 else None)
                    kw_mmdone[bi] = st.n
                    p.w("scalar", st, kw_mmdone[bi])
                    if oi >= 2:
                        p.w("scalar", do, ost_dma[oi - 2])
                    p.o("scalar", lambda E, opar=opar: E.copy(
                        slc(ost, opar, TC), slc(ps_o, opar, TC)), ss)
                    out_evac[oi] = ss.n
                    ost_dma[oi] = p.dma(
                        "scalar",
                        dt["mout"][mt * 128:(mt + 1) * 128, c * TC:(c + 1) * TC],
                        slc(ost, opar, TC), do)
                wout_done_c[c] = st.n
                deps["mout_c"][c] = do.n

            for c in range(NCH):
                if c > 0:
                    p.w("tensor", ss, bsb_evac[c - 1])
                    if ("my_dbg" in out_names or "bc_dbg" in out_names):
                        p.w("scalar", dm)
                        p.w("vector", dm)
                for k in range(NDI):
                    p.o("tensor", lambda E, k=k, c=c: E.matmul(
                        ps_sb[:, :], slc(wb_sb, k, NST), uslc(k, c),
                        start=(k == 0), stop=(k == NDI - 1)))
                for k in range(NDI):
                    p.o("tensor", lambda E, k=k, c=c: E.matmul(
                        ps_sc[:, :], slc(wc_sb, k, NST), uslc(k, c),
                        start=(k == 0), stop=(k == NDI - 1)),
                        st if k == NDI - 1 else None)
                p.w("scalar", st)
                p.o("scalar", lambda E: E.copy(B_sb[:, :], ps_sb[:, :]), ss)
                p.o("scalar", lambda E: E.copy(C_sb[:, :], ps_sc[:, :]), ss)
                bsb_evac[c] = ss.n
                p.w("tensor", ss, bsb_evac[c])
                bc_ev = {}
                for i in range(2 * NST):
                    n = i % NST
                    src = B_sb if i < NST else C_sb
                    dst = Bbc if i < NST else Cbc
                    bpar = i % 2
                    if i >= 2:
                        p.w("tensor", sv, bc_ev[i - 2])
                    elif c > 0:
                        p.w("tensor", sv, bc_evac_c[c - 1])
                    p.o("tensor", lambda E, n=n, src=src, bpar=bpar: E.matmul(
                        slc(ps_br, bpar, TC), cons["sel_bf"][:, n * 128:(n + 1) * 128],
                        src[:, :], start=True, stop=True), st)
                    p.w("vector", st)
                    p.o("vector", lambda E, n=n, dst=dst, bpar=bpar:
                        E.tensor_copy(slc(dst, n, TC), slc(ps_br, bpar, TC)),
                        sv)
                    bc_ev[i] = sv.n
                bc_evac_c[c] = sv.n
                sres_load(0, c)
                if c > 0:
                    emit_wout(c - 1)

                if c == 0:
                    kw_block(0, blk_src(0))
                for dc in range(NDI):
                    bi = c * (NDI + NDT) + dc
                    kw_block(bi + 1, blk_src(bi + 1))
                    if dc + 1 < NDI:
                        sres_load(dc + 1, c)
                    par = dc % 2
                    p.w("tensor", ss, kw_cast[bi])
                    if bi >= 2:
                        p.w("tensor", sv, psd_free.get(bi - 2, 0))
                    for k in range(NDI):
                        p.o("tensor", lambda E, par=par, bip=bi % 3, k=k, c=c:
                            E.matmul(slc(ps_d, par, TC),
                                     kw_bf[:, bip * 2048 + k * 128:
                                           bip * 2048 + (k + 1) * 128],
                                     uslc(k, c), start=(k == 0),
                                     stop=(k == NDI - 1)),
                            st if k == NDI - 1 else None)
                    kw_mmdone[bi] = st.n
                    dmm = st.n
                    p.w("vector", st, dmm)
                    p.o("vector", lambda E, par=par, dc=dc: E.tensor_scalar(
                        sp_e[:, :], slc(ps_d, par, TC), bd_sb[:, dc:dc + 1],
                        None, ALU.add), sv)
                    psd_free[bi] = sv.n
                    p.o("vector", lambda E: E.scalar_tensor_tensor(
                        sp_t1[:, :], sp_e[:, :], -1.0, sp_e[:, :],
                        ALU.mult, ALU.max))
                    p.o("vector", lambda E: E.tensor_single_scalar(
                        sp_e[:, :], sp_e[:, :], 0.0, ALU.max), sv)
                    v_abs = sv.n
                    p.w("scalar", sv, v_abs)
                    p.o("scalar", lambda E: E.activation(
                        sp_t2[:, :], sp_t1[:, :], AF.Exp, scale=-1.0))
                    p.o("scalar", lambda E: E.activation(
                        sp_t1[:, :], sp_t2[:, :], AF.Ln,
                        bias=cons["cvals"][:, 1:2]), ss)
                    s_ln = ss.n
                    p.w("vector", ss, s_ln)
                    if "del_dbg" in out_names and (c, dc) != (0, 0):
                        pk = (c, dc - 1) if dc > 0 else (c - 1, NDI - 1)
                        p.w("vector", dm, dbg_del[pk])
                    p.o("vector", lambda E: E.tensor_add(
                        d16[:, :], sp_t1[:, :], sp_e[:, :]), sv)
                    v_d16 = sv.n
                    p.o("vector", lambda E, dc=dc, c=c: E.tensor_mul(
                        dub[:, :], d16[:, :], uslc(dc, c)), sv)
                    v_du = sv.n
                    if "del_dbg" in out_names:
                        p.w("gpsimd", sv, v_du)
                        dbg_del[(c, dc)] = p.dma(
                            "gpsimd",
                            dt["del_dbg"][dc * 128:(dc + 1) * 128,
                                          c * TC:(c + 1) * TC],
                            d16[:, :], dm)
                    for n in range(NST):
                        gi = len(scan_hist)
                        apar = gi % 4
                        bpar = gi % 4
                        p.w("scalar", sv,
                            max(v_d16, scan_hist[gi - 4] if gi >= 4 else 0))
                        p.o("scalar", lambda E, apar=apar, dc=dc, n=n:
                            E.activation(slc(dab, apar, TC), d16[:, :], AF.Exp,
                                         scale=a_sb[:, dc * NST + n:
                                                    dc * NST + n + 1]), ss)
                        da_done = ss.n
                        p.o("vector", lambda E, bpar=bpar, n=n: E.tensor_mul(
                            slc(bb, bpar, TC), dub[:, :], slc(Bbc, n, TC)))
                        p.w("vector", ss, da_done)
                        p.o("vector", lambda E, apar=apar, bpar=bpar, dc=dc, n=n:
                            E.tensor_tensor_scan(
                                slc(hs, bpar, TC), slc(dab, apar, TC),
                                slc(bb, bpar, TC),
                                hstate[:, dc * NST + n: dc * NST + n + 1],
                                ALU.mult, ALU.add), sv)
                        scan_hist.append(sv.n)
                        if n % 2 == 1:
                            lo = (gi - 1) % 4
                            p.o("vector", lambda E, lo=lo, n=n: E.tensor_mul(
                                hcb[:, lo * TC:(lo + 2) * TC],
                                hs[:, lo * TC:(lo + 2) * TC],
                                Cbc[:, (n - 1) * TC:(n + 1) * TC]))
                        if n % 4 == 3:
                            grp = n // 4
                            hsr = hs.rearrange("p (n t) -> p n t", n=4)
                            p.o("vector", lambda E, dc=dc, n=n, hsr=hsr:
                                E.tensor_copy(
                                    hstate[:, dc * NST + n - 3:
                                           dc * NST + n + 1],
                                    hsr[:, :, TC - 1:TC].rearrange(
                                        "p n t -> p (n t)")))
                            p.o("vector", lambda E: E.tensor_add(
                                yp12[:, :], hcb[:, 0:2 * TC],
                                hcb[:, 2 * TC:4 * TC]))
                            if grp == 0:
                                p.o("vector", lambda E: E.tensor_add(
                                    yd[:, :], yp12[:, 0:TC], yp12[:, TC:2 * TC]),
                                    sv)
                            else:
                                p.o("vector", lambda E: E.tensor_add(
                                    yp[:, :], yp12[:, 0:TC], yp12[:, TC:2 * TC]))
                                p.o("vector", lambda E: E.tensor_add(
                                    yd[:, :], yd[:, :], yp[:, :]), sv)
                    if dc == 0 and c > 0:
                        p.w("vector", st, wout_done_c[c - 1])
                    p.o("vector", lambda E, dc=dc, c=c: E.tensor_scalar_mul(
                        dub[:, :], uslc(dc, c), dp_sb[:, dc:dc + 1]))
                    p.o("vector", lambda E: E.tensor_add(
                        sp_e[:, :], yd[:, :], dub[:, :]))
                    p.w("vector", dm, sres_dma[c * NDI + dc])
                    p.o("vector", lambda E, dc=dc: E.tensor_mul(
                        slc(myb, dc, TC), sp_e[:, :], slc(srb, dc % 2, TC)), sv)
                    my_dc_done[c * NDI + dc] = sv.n
                my_done_c[c] = sv.n
                if "my_dbg" in out_names:
                    p.w("gpsimd", sv, my_done_c[c])
                    for dcq in range(NDI):
                        p.dma("gpsimd",
                              dt["my_dbg"][dcq * 128:(dcq + 1) * 128,
                                           c * TC:(c + 1) * TC],
                              slc(myb, dcq, TC), dm)
                if "bc_dbg" in out_names:
                    p.w("gpsimd", sv, bc_evac_c[c])
                    p.dma("gpsimd", dt["bc_dbg"][0:16, c * TC:(c + 1) * TC],
                          B_sb[:, :], dm)
                    p.dma("gpsimd", dt["bc_dbg"][16:32, c * TC:(c + 1) * TC],
                          C_sb[:, :], dm)
                dbg_dm_c = dm.n
            emit_wout(NCH - 1)
            p.w("gpsimd", do)
            p.emit(nc)
    mstack.close()

    for PH, xin_src, second_in, xout_dst in (
            ("P4", "xin", "mout", "x1"),
            ("P6", "x2", None, "out")):
        if PH not in phases:
            continue
        emit_ffn(nc, dt, cons, sems, deps, PH, xin_src, second_in, xout_dst)

        if PH == "P4" and "P5" in phases:
            emit_attn(nc, dt, cons, sems, deps)

    stack.close()
    return nc, dt


def _sbp(nc, st, name, shape, dtype):
    return st.enter_context(nc.sbuf_tensor(name, shape, dtype)).ap()


def _psp(nc, st, name, shape, dtype=F32):
    return st.enter_context(nc.psum_tensor(name, shape, dtype)).ap()


def alloc_ln_bufs(nc, st, W, tag):
    return {
        "xb": _sbp(nc, st, f"ln_xb_{tag}", [128, NDT * W], BF16),
        "xsq": _sbp(nc, st, f"ln_xsq_{tag}", [128, NDT * W], BF16),
        "t": _sbp(nc, st, f"ln_t_{tag}", [128, W], F32),
        "srow": _sbp(nc, st, f"ln_srow_{tag}", [1, 6 * W], F32),
        "sum": _psp(nc, st, f"ln_sum_{tag}", [1, W]),
        "ssq": _psp(nc, st, f"ln_ssq_{tag}", [1, W]),
        "bc": _psp(nc, st, f"ln_bc_{tag}", [128, 2 * W]),
    }


def emit_ln(p, sems, x_ap, dst, W, cons, b, psum_guard=None, x_gate=None):
    sv, ss, st = sems["sv"], sems["ss"], sems["st"]
    if x_gate is not None:
        p.w("vector", x_gate[0], x_gate[1])
        p.w("scalar", x_gate[0], x_gate[1])

    def sr(i):
        return b["srow"][0:1, i * W:(i + 1) * W]
    if psum_guard is not None:
        for gc, gvv in psum_guard:
            p.w("tensor", gc, gvv)
    for a in range(NDT):
        p.o("vector", lambda E, a=a: E.tensor_copy(
            slc(b["xb"], a, W), slc(x_ap, a, W)), sv)
        p.o("scalar", lambda E, a=a: E.square(
            slc(b["xsq"], a, W), slc(x_ap, a, W)), ss)
        p.w("tensor", sv)
        p.w("tensor", ss)
        p.o("tensor", lambda E, a=a: E.matmul(
            b["sum"][:, :], cons["ones_col_bf"][:, :], slc(b["xb"], a, W),
            start=(a == 0), stop=(a == NDT - 1)))
        p.o("tensor", lambda E, a=a: E.matmul(
            b["ssq"][:, :], cons["ones_col_bf"][:, :], slc(b["xsq"], a, W),
            start=(a == 0), stop=(a == NDT - 1)),
            st if a == NDT - 1 else None)
    v_cast, s_sq = sv.n, ss.n
    t_stats = st.n
    p.w("vector", st, t_stats)
    p.o("vector", lambda E: E.tensor_scalar_mul(sr(0), b["sum"][0:1, :], 1.0 / D))
    p.o("vector", lambda E: E.tensor_mul(sr(1), sr(0), sr(0)))
    p.o("vector", lambda E: E.scalar_tensor_tensor(
        sr(2), b["ssq"][0:1, :], 1.0 / D, sr(1), ALU.mult, ALU.subtract), sv)
    v_var = sv.n
    p.w("scalar", sv, v_var)
    p.o("scalar", lambda E: E.activation(sr(3), sr(2), AF.Ln, bias=cons["cvals"][0:1, 0:1]))
    p.o("scalar", lambda E: E.activation(sr(4), sr(3), AF.Exp, scale=-0.5), ss)
    s_rstd = ss.n
    p.w("vector", ss, s_rstd)
    p.o("vector", lambda E: E.tensor_mul(sr(5), sr(0), sr(4)), sv)
    v_mrs = sv.n
    p.w("tensor", sv, v_mrs)
    p.o("tensor", lambda E: E.matmul(
        b["bc"][:, 0:W], cons["ones_row_f32"][:, :], sr(4), start=True,
        stop=True))
    p.o("tensor", lambda E: E.matmul(
        b["bc"][:, W:2 * W], cons["ones_row_f32"][:, :], sr(5), start=True,
        stop=True), st)
    t_bc = st.n
    p.w("vector", st, t_bc)
    for a in range(NDT):
        p.o("vector", lambda E, a=a: E.tensor_mul(
            b["t"][:, :], slc(x_ap, a, W), b["bc"][:, 0:W]))
        p.o("vector", lambda E, a=a: E.tensor_sub(
            dst(a), b["t"][:, :], b["bc"][:, W:2 * W]), sv)
    return sv.n


def emit_ffn(nc, dt, cons, sems, deps, tag, xin_src, second_in, xout_dst):
    sv, ss, st = sems["sv"], sems["ss"], sems["st"]
    dw, dx, do, dm = sems["dw"], sems["dx"], sems["do"], sems["dm"]
    in_dep_gate = (deps["mout_c"][NCH - 1] if second_in
                   else deps["x2_c"][NCH - 1])
    with contextlib.ExitStack() as ph:
        p = Plan()
        w1t8 = _sbp(nc, ph, f"{tag}_w1", [128, 8 * H4], FP8)
        w2t8 = _sbp(nc, ph, f"{tag}_w2", [128, NH4 * D], FP8)
        b1s = _sbp(nc, ph, f"{tag}_b1", [128, NH4], F32)
        b2s = _sbp(nc, ph, f"{tag}_b2", [128, NDT], F32)
        for geng in ("sync", "scalar", "gpsimd"):
            p.w(geng, do, in_dep_gate)
        p.dma("sync", b1s[:, :], dt["b1_t"][:, :], dw)
        p.dma("sync", b2s[:, :], dt["b2_t"][:, :], dw)
        b_ready = dw.n
        prev_v, prev_s = sv.n, ss.n
        x0b = _sbp(nc, ph, f"{tag}_x0", [128, 2 * NDT * TC], F32)
        mob = (_sbp(nc, ph, f"{tag}_mo", [128, NDT * TC], BF16)
               if second_in else None)
        dmb = _sbp(nc, ph, f"{tag}_dm", [128, NDT * TC], BF16)
        x_c = (_sbp(nc, ph, f"{tag}_xc", [128, NDT * TC], F32)
               if second_in else None)
        xn8 = _sbp(nc, ph, f"{tag}_xn8", [128, NDT * TC], FP8)
        lnb = alloc_ln_bufs(nc, ph, TC, tag)

        xin_r = dt[xin_src].rearrange("(a p) t -> p a t", p=128)
        mo_r = dt[second_in].rearrange("(a p) t -> p a t", p=128) \
            if second_in else None
        dm_r = dt["dropm"].rearrange("(a p) t -> p a t", p=128)
        out_r = dt[xout_dst].rearrange("(a p) t -> p a t", p=128)
        asm_done = [0] * NCH
        xread_done = [0] * NCH
        drops_done = {}
        outd = [0] * NCH

        def front_end(c):
            slot = c % 2
            if c >= 2:
                p.w("sync", sv, xread_done[c - 2])
            if xin_src == "x2":
                p.w("sync", do, deps["x2_c"][c])
            p.dma("sync", slc(x0b, slot, NDT * TC).rearrange(
                "p (a t) -> p a t", a=NDT),
                xin_r[:, :, c * TC:(c + 1) * TC], dx)
            if second_in:
                if c >= 1:
                    p.w("sync", sv, asm_done[c - 1])
                p.w("sync", do, deps["mout_c"][c])
                p.dma("sync", mob.rearrange("p (a t) -> p a t", a=NDT),
                      mo_r[:, :, c * TC:(c + 1) * TC], dx)
            if c >= 1:
                p.w("sync", sv, drops_done.get((c - 1, NDT - 1), 0))
            p.dma("sync", dmb.rearrange("p (a t) -> p a t", a=NDT),
                  dm_r[:, :, c * TC:(c + 1) * TC], dx)
            p.w("vector", dx)
            if c >= 1:
                p.w("vector", do, outd[c - 1])
            if second_in:
                for a in range(NDT):
                    p.o("vector", lambda E, a=a, slot=slot: E.tensor_add(
                        slc(x_c, a, TC),
                        x0b[:, slot * NDT * TC + a * TC:
                            slot * NDT * TC + (a + 1) * TC],
                        slc(mob, a, TC)), sv)
                asm_done[c] = sv.n
                x_ap, xoff = x_c, 0
            else:
                x_ap, xoff = x0b, slot * NDT * TC
            xgate = (sems["sv"], asm_done[c]) if second_in \
                else (dx, dx.n)
            if c == 0:
                p.w("tensor", sv, prev_v)
                p.w("tensor", ss, prev_s)
            xn_done = emit_ln(
                p, sems, x_ap[:, xoff:xoff + NDT * TC],
                lambda a: slc(xn8, a, TC), TC, cons, lnb, x_gate=xgate)
            return xn_done, x_ap, xoff

        fe0 = front_end(0)
        wcast = {}
        with contextlib.ExitStack() as prep:
            stage = _sbp(nc, prep, f"{tag}_wst", [128, 2 * H4], F32)
            QROT = (("sync", dw), ("scalar", do), ("gpsimd", dm))
            for i in range(8 + 8):
                slot = i % 2
                q, qs = QROT[i % 3]
                if i >= 2:
                    p.w(q, wcast[i - 2][0], wcast[i - 2][1])
                if i < 8:
                    srcw = dt["w1_t"][i * 128:(i + 1) * 128, :]
                    dst = w1t8[:, i * H4:(i + 1) * H4]
                else:
                    g = i - 8
                    srcw = dt["w2_t"][g * 512:(g + 1) * 512, :].rearrange(
                        "(k p) d -> p k d", p=128)
                    dst = w2t8[:, g * H4:(g + 1) * H4]
                width = H4
                sview = stage[:, slot * H4: slot * H4 + width]
                p.dma(q, sview.rearrange("p (k d) -> p k d", k=4)
                      if i >= 8 else sview, srcw, qs)
                ceng = "vector" if i % 2 == 0 else "scalar"
                csem = sv if i % 2 == 0 else ss
                p.w(ceng, qs)
                if ceng == "vector":
                    p.o(ceng, lambda E, dst=dst, slot=slot, width=width:
                        E.tensor_copy(dst,
                                      stage[:, slot * H4: slot * H4 + width]),
                        csem)
                else:
                    p.o(ceng, lambda E, dst=dst, slot=slot, width=width:
                        E.copy(dst, stage[:, slot * H4: slot * H4 + width]),
                        csem)
                wcast[i] = (csem, csem.n)
        w8_ready = sv.n
        w8_ready_ss = ss.n
        h8 = _sbp(nc, ph, f"{tag}_h8", [128, NH4 * TC], FP8)
        h16b = _sbp(nc, ph, f"{tag}_h16", [128, 2 * TC], F16)
        y16b = _sbp(nc, ph, f"{tag}_y16", [128, 2 * TC], F16)
        dtmp = _sbp(nc, ph, f"{tag}_dt", [128, TC], F32)
        out_c = _sbp(nc, ph, f"{tag}_oc", [128, NDT * TC], F32)
        ps_h = _psp(nc, ph, f"{tag}_psh", [128, 2 * TC])
        ps_y = _psp(nc, ph, f"{tag}_psy", [128, 2 * TC])

        for c in range(NCH):
            xn_done, x_ap, xoff = fe0 if c == 0 else front_end(c)
            if tag == "P4" and "xn8_dbg" in dt:
                p.w("gpsimd", sv, xn_done)
                for a in range(NDT):
                    p.dma("gpsimd",
                          dt["xn8_dbg"][a * 128:(a + 1) * 128,
                                        c * TC:(c + 1) * TC],
                          slc(xn8, a, TC), dm)
            p.w("tensor", sv, max(xn_done, w8_ready))
            if c == 0:
                p.w("tensor", ss, w8_ready_ss)
            h_evac = {}
            h8c = {}
            for m in range(NH4):
                hp = m % 2
                if m >= 2:
                    p.w("tensor", ss, h_evac[m - 2])
                w1_3 = w1t8.rearrange("p (k h) -> p k h", k=8)
                xn_3 = xn8.rearrange("p (k t) -> p k t", k=8)
                for kk in range(4):
                    p.o("tensor", lambda E, m=m, kk=kk, hp=hp, w1_3=w1_3,
                        xn_3=xn_3: E.matmul(
                        slc(ps_h, hp, TC),
                        w1_3[:, 2 * kk:2 * kk + 2, m * 128:(m + 1) * 128],
                        xn_3[:, 2 * kk:2 * kk + 2, :],
                        start=(kk == 0), stop=(kk == 3),
                        perf_mode=mybir.MatmulPerfMode.DoubleRow),
                        st if kk == 3 else None)
                p.w("scalar", st)
                if m == 0 and c == 0:
                    p.w("scalar", dw, b_ready)
                if m >= 2:
                    p.w("scalar", sv, h8c[m - 2])
                p.o("scalar", lambda E, m=m, hp=hp: E.activation(
                    slc(h16b, hp, TC), slc(ps_h, hp, TC), AF.Relu,
                    bias=b1s[:, m:m + 1]), ss)
                h_evac[m] = ss.n
                p.w("vector", ss)
                p.o("vector", lambda E, m=m, hp=hp: E.tensor_copy(
                    slc(h8, m, TC), slc(h16b, hp, TC)), sv)
                h8c[m] = sv.n
            if tag == "P4" and "h8_dbg" in dt:
                p.w("gpsimd", sv, h8c[NH4 - 1])
                for m in range(NH4):
                    p.dma("gpsimd",
                          dt["h8_dbg"][m * 128:(m + 1) * 128,
                                       c * TC:(c + 1) * TC],
                          slc(h8, m, TC), dm)
                p.w("vector", dm)
            p.w("tensor", sv, h8c[NH4 - 1])
            y_evac = {}
            for nt in range(NDT):
                yp_ = nt % 2
                if nt >= 2:
                    p.w("tensor", ss, y_evac[nt - 2])
                w2_3 = w2t8.rearrange("p (k d) -> p k d", k=NH4)
                h8_3 = h8.rearrange("p (k t) -> p k t", k=NH4)
                for kk in range(NH4 // 2):
                    p.o("tensor", lambda E, nt=nt, kk=kk, yp_=yp_, w2_3=w2_3,
                        h8_3=h8_3: E.matmul(
                        slc(ps_y, yp_, TC),
                        w2_3[:, 2 * kk:2 * kk + 2, nt * 128:(nt + 1) * 128],
                        h8_3[:, 2 * kk:2 * kk + 2, :],
                        start=(kk == 0), stop=(kk == NH4 // 2 - 1),
                        perf_mode=mybir.MatmulPerfMode.DoubleRow),
                        st if kk == NH4 // 2 - 1 else None)
                p.w("scalar", st)
                if nt >= 2:
                    p.w("scalar", sv, drops_done[(c, nt - 2)])
                p.o("scalar", lambda E, nt=nt, yp_=yp_: E.activation(
                    slc(y16b, yp_, TC), slc(ps_y, yp_, TC), AF.Identity,
                    bias=b2s[:, nt:nt + 1]), ss)
                y_evac[nt] = ss.n
                p.w("vector", ss)
                p.o("vector", lambda E, nt=nt, yp_=yp_: E.tensor_mul(
                    dtmp[:, :], slc(y16b, yp_, TC), slc(dmb, nt, TC)))
                p.o("vector", lambda E, nt=nt, xoff=xoff: E.tensor_add(
                    slc(out_c, nt, TC),
                    x_ap[:, xoff + nt * TC: xoff + (nt + 1) * TC],
                    dtmp[:, :]), sv)
                drops_done[(c, nt)] = sv.n
            xread_done[c] = sv.n
            p.w("scalar", sv)
            outd[c] = p.dma(
                "scalar", out_r[:, :, c * TC:(c + 1) * TC],
                out_c.rearrange("p (a t) -> p a t", a=NDT), do)
            if xout_dst == "x1":
                deps["x1_c"][c] = outd[c]
        p.w("gpsimd", do)
        p.emit(nc)


def emit_attn(nc, dt, cons, sems, deps):
    sv, ss, st = sems["sv"], sems["ss"], sems["st"]
    dw, dx, do, dm = sems["dw"], sems["dx"], sems["do"], sems["dm"]
    with contextlib.ExitStack() as ph:
        p = Plan()
        wq = _sbp(nc, ph, "p5_wq", [128, 8 * D], BF16)
        wk = _sbp(nc, ph, "p5_wk", [128, 8 * D], BF16)
        wv = _sbp(nc, ph, "p5_wv", [128, 8 * D], BF16)
        wo = _sbp(nc, ph, "p5_wo", [128, 8 * D], BF16)
        bqs = _sbp(nc, ph, "p5_bq", [128, NDT], F32)
        bks = _sbp(nc, ph, "p5_bk", [128, NDT], F32)
        bos = _sbp(nc, ph, "p5_bo", [128, NDT], F32)
        bvf = _sbp(nc, ph, "p5_bvf", [1, D], F32)
        bvb = _sbp(nc, ph, "p5_bvb", [1, D], BF16)
        for geng in ("sync", "scalar", "gpsimd"):
            p.w(geng, do, deps["x1_c"][NCH - 1])
        p.dma("sync", bqs[:, :], dt["bq_t"][:, :], dw)
        p.dma("sync", bks[:, :], dt["bk_t"][:, :], dw)
        p.dma("sync", bos[:, :], dt["bo_t"][:, :], dw)
        p.dma("sync", bvf[:, :], dt["bv_row"][:, :], dw)
        b_ready = dw.n
        prev_v, prev_s = sv.n, ss.n
        x1b = _sbp(nc, ph, "p5_x1", [128, 2 * NDT * TC], F32)
        xnb = _sbp(nc, ph, "p5_xn", [128, NDT * TC], BF16)
        fq = _sbp(nc, ph, "p5_fq", [128, NDT * TC], BF16)
        fk = _sbp(nc, ph, "p5_fk", [128, NDT * TC], BF16)
        lnt = _sbp(nc, ph, "p5_lnt", [128, TC], F32)
        lsr = _sbp(nc, ph, "p5_lsr", [1, 6 * TC], F32)
        ps_A = _psp(nc, ph, "p5_psA", [128, 2 * TC])
        lnb = {"xb": fk, "xsq": fq, "t": lnt, "srow": lsr,
               "sum": ps_A[0:1, 0:TC], "ssq": ps_A[0:1, TC:2 * TC],
               "bc": ps_A}
        x1_r = dt["x1"].rearrange("(a p) t -> p a t", p=128)
        x2_r = dt["x2"].rearrange("(a p) t -> p a t", p=128)
        p.dma("sync", slc(x1b, 0, NDT * TC).rearrange(
            "p (a t) -> p a t", a=NDT), x1_r[:, :, 0:TC], dx)
        p.w("vector", dx)
        xn0 = emit_ln(
            p, sems, x1b[:, 0:NDT * TC],
            lambda a: slc(xnb, a, TC), TC, cons, lnb,
            psum_guard=[(sv, prev_v), (ss, prev_s)], x_gate=(dx, dx.n))
        wcast = {}
        with contextlib.ExitStack() as prep:
            stage = _sbp(nc, prep, "p5_wst", [128, 2 * 2 * D], F32)
            for i, (wsb, wnm) in enumerate(
                    [(wq, "wq_t"), (wk, "wk_t"), (wv, "wv_t"), (wo, "wo_t")]):
                for g in range(4):
                    j = i * 4 + g
                    slot = j % 2
                    q, qs = (("sync", dw), ("scalar", do),
                             ("gpsimd", dm))[j % 3]
                    if j >= 2:
                        p.w(q, wcast[j - 2][0], wcast[j - 2][1])
                    p.dma(q, stage[:, slot * 2 * D:(slot + 1) * 2 * D]
                          .rearrange("p (k d) -> p k d", k=2),
                          dt[wnm][g * 256:(g + 1) * 256, :].rearrange(
                              "(k p) d -> p k d", p=128), qs)
                    ceng = "vector" if j % 2 == 0 else "scalar"
                    csem = sv if j % 2 == 0 else ss
                    p.w(ceng, qs)
                    if ceng == "vector":
                        p.o(ceng, lambda E, wsb=wsb, g=g, slot=slot:
                            E.tensor_copy(
                                wsb[:, g * 2 * D:(g + 1) * 2 * D],
                                stage[:, slot * 2 * D:(slot + 1) * 2 * D]),
                            csem)
                    else:
                        p.o(ceng, lambda E, wsb=wsb, g=g, slot=slot:
                            E.copy(
                                wsb[:, g * 2 * D:(g + 1) * 2 * D],
                                stage[:, slot * 2 * D:(slot + 1) * 2 * D]),
                            csem)
                    wcast[j] = (csem, csem.n)
        p.w("vector", dw, b_ready)
        p.o("vector", lambda E: E.tensor_copy(bvb[:, :], bvf[:, :]), sv)
        w_ready = sv.n
        w_ready_ss = ss.n
        vtm = _sbp(nc, ph, "p5_vtm", [128, NSUB * D], BF16)
        ktm = _sbp(nc, ph, "p5_ktm", [128, NSUB * D], BF16)
        aos = _sbp(nc, ph, "p5_aos", [128, D], BF16)
        aoT = _sbp(nc, ph, "p5_aoT", [128, NDT * TC], BF16)
        out_c = _sbp(nc, ph, "p5_oc", [128, NDT * TC], F32)
        tq1 = _sbp(nc, ph, "p5_t1", [128, TC], F32)
        tq2 = _sbp(nc, ph, "p5_t2", [128, TC], F32)
        tq3 = _sbp(nc, ph, "p5_t3", [128, TC], F32)
        smb = _sbp(nc, ph, "p5_sm", [128, 2 * SUB], BF16)
        kvb = _sbp(nc, ph, "p5_kv", [128, NDT * DH], BF16)
        ksf = _sbp(nc, ph, "p5_ksf", [128, NDT], F32)
        kst = _sbp(nc, ph, "p5_kst", [128, NDT], F32)
        ksb = _sbp(nc, ph, "p5_ksb", [128, NDT], BF16)
        rall = _sbp(nc, ph, "p5_ra", [128, HEADS], F32)
        rinv = _sbp(nc, ph, "p5_ri", [128, HEADS], F32)
        ps_s0 = _psp(nc, ph, "p5_pss0", [128, SUB])
        ps_s1 = _psp(nc, ph, "p5_pss1", [128, SUB])
        ps_t0 = _psp(nc, ph, "p5_pst0", [128, SUB], BF16)
        ps_t1 = _psp(nc, ph, "p5_pst1", [128, SUB], BF16)
        ps_den = _psp(nc, ph, "p5_psd", [128, HEADS])
        ps_kv = _psp(nc, ph, "p5_pskv", [128, NDT * DH])

        p.o("vector", lambda E: E.memset(ksf[:, :], 0.0))
        p.o("vector", lambda E: E.memset(ksb[:, :], 0.0))
        p.o("vector", lambda E: E.memset(kvb[:, :], 0.0), sv)

        a_rd = w_ready
        a_rd_ss = 0
        a_wr = 0
        fk_rd = 0
        kv_ev = [0] * (NCH * NSUB + 1)
        xread = [0] * NCH
        outd = [0] * NCH
        first_kv = True
        for sc in range(NCH):
            slot = sc % 2
            xoff = slot * NDT * TC
            if sc == 0:
                p.w("scalar", dw, b_ready)
                xn_done = xn0
            else:
                if sc >= 2:
                    p.w("sync", sv, xread[sc - 2])
                p.w("sync", do, deps["x1_c"][sc])
                p.dma("sync", slc(x1b, slot, NDT * TC).rearrange(
                    "p (a t) -> p a t", a=NDT),
                    x1_r[:, :, sc * TC:(sc + 1) * TC], dx)
                p.w("vector", dx)
                p.w("vector", st, fk_rd)
                xn_done = emit_ln(
                    p, sems, x1b[:, xoff:xoff + NDT * TC],
                    lambda a: slc(xnb, a, TC), TC, cons, lnb,
                    psum_guard=[(sv, a_rd), (ss, a_rd_ss)], x_gate=(dx, dx.n))
            a_wr = st.n
            a_rd = xn_done
            pe_gate = max(xn_done, w_ready)
            p.w("tensor", sv, pe_gate)
            if sc == 0:
                p.w("tensor", ss, w_ready_ss)
            prev_ev = {}
            prev_relu = {}
            for i in range(2 * NDT):
                m = i % NDT
                wsb, bias, dst = (wq, bqs, fq) if i < NDT else (wk, bks, fk)
                pp = i % 2
                if i >= 2:
                    p.w("tensor", sv, prev_ev[i - 2])
                    p.w("tensor", ss, prev_relu[i - 2])
                for k in range(NDT):
                    p.o("tensor", lambda E, wsb=wsb, m=m, k=k, pp=pp: E.matmul(
                        slc(ps_A, pp, TC),
                        wsb[:, k * D + m * 128: k * D + (m + 1) * 128],
                        slc(xnb, k, TC), start=(k == 0), stop=(k == NDT - 1)),
                        st if k == NDT - 1 else None)
                p.w("scalar", st)
                if i >= 1:
                    p.w("scalar", sv, prev_ev[i - 1])
                p.o("scalar", lambda E, bias=bias, m=m, pp=pp: E.activation(
                    tq1[:, :], slc(ps_A, pp, TC), AF.Relu,
                    bias=bias[:, m:m + 1]), ss)
                prev_relu[i] = ss.n
                p.w("vector", st)
                p.w("vector", ss)
                p.o("vector", lambda E, bias=bias, m=m, pp=pp: E.tensor_scalar(
                    tq2[:, :], slc(ps_A, pp, TC), bias[:, m:m + 1], 0.0,
                    ALU.add, ALU.min), sv)
                p.w("scalar", sv)
                p.o("scalar", lambda E: E.activation(
                    tq3[:, :], tq2[:, :], AF.Exp), ss)
                p.w("vector", ss)
                p.o("vector", lambda E, dst=dst, m=m: E.tensor_add(
                    slc(dst, m, TC), tq1[:, :], tq3[:, :]), sv)
                prev_ev[i] = sv.n
            qk_done = sv.n
            a_wr = st.n
            a_rd = qk_done
            a_rd_ss = ss.n
            if "fq_dbg" in dt:
                p.w("gpsimd", sv, qk_done)
                for m in range(NDT):
                    p.dma("gpsimd", dt["fq_dbg"][m * 128:(m + 1) * 128,
                                                 sc * TC:(sc + 1) * TC],
                          slc(fq, m, TC), dm)
                    p.dma("gpsimd", dt["fk_dbg"][m * 128:(m + 1) * 128,
                                                 sc * TC:(sc + 1) * TC],
                          slc(fk, m, TC), dm)
            vt_ev = {}
            for s in range(NSUB):
                for nh in range(2):
                    j = s * 2 + nh
                    pp = j % 2
                    if j >= 2:
                        p.w("tensor", sv, vt_ev[j - 2])
                    else:
                        p.w("tensor", sv, prev_ev[2 * NDT - 2 + j])
                        p.w("tensor", ss, prev_relu[2 * NDT - 2 + j])
                    for k in range(NDT):
                        p.o("tensor", lambda E, s=s, nh=nh, k=k, pp=pp:
                            E.matmul(
                                slc(ps_A, pp, TC),
                                xnb[:, k * TC + s * SUB: k * TC + (s + 1) * SUB],
                                wv[:, k * D + nh * TC: k * D + (nh + 1) * TC],
                                start=(k == 0), stop=False))
                    p.o("tensor", lambda E, nh=nh, pp=pp: E.matmul(
                        slc(ps_A, pp, TC), cons["ones_row_bf"][:, :],
                        bvb[0:1, nh * TC:(nh + 1) * TC],
                        start=False, stop=True), st)
                    p.w("vector", st)
                    p.o("vector", lambda E, s=s, nh=nh, pp=pp: E.tensor_copy(
                        vtm[:, s * D + nh * TC: s * D + (nh + 1) * TC],
                        slc(ps_A, pp, TC)), sv)
                    vt_ev[j] = sv.n
            a_wr = st.n
            a_rd = sv.n
            if "vtm_dbg" in dt:
                p.w("gpsimd", sv)
                for s_ in range(NSUB):
                    p.dma("gpsimd",
                          dt["vtm_dbg"][sc * TC + s_ * SUB:
                                        sc * TC + (s_ + 1) * SUB, :],
                          vtm[:, s_ * D:(s_ + 1) * D], dm)
            p.w("tensor", sv, qk_done)
            tr_ev = {}
            for j in range(NDT * NSUB):
                m, s = j % NDT, j // NDT
                pst = ps_t0 if j % 2 == 0 else ps_t1
                if j >= 2:
                    p.w("tensor", sv, tr_ev[j - 2])
                p.o("tensor", lambda E, m=m, s=s, pst=pst: E.transpose(
                    pst[:, :], fk[:, m * TC + s * SUB: m * TC + (s + 1) * SUB],
                    cons["ident_bf"][:, :]), st)
                p.w("vector", st)
                p.o("vector", lambda E, m=m, s=s, pst=pst: E.tensor_copy(
                    ktm[:, s * D + m * 128: s * D + (m + 1) * 128],
                    pst[:, :]), sv)
                tr_ev[j] = sv.n
            ktm_done = sv.n
            for s in range(NSUB):
                gsub = sc * NSUB + s
                p.w("tensor", sv, max(ktm_done, kv_ev[gsub], a_rd))
                sm_done = {}
                num_done = {}
                s_mm = {}

                def head_slices(h):
                    fkh = fk[64 * (h % 2):64 * (h % 2) + 64,
                             (h // 2) * TC + s * SUB:
                             (h // 2) * TC + (s + 1) * SUB]
                    fqh = fq[64 * (h % 2):64 * (h % 2) + 64,
                             (h // 2) * TC + s * SUB:
                             (h // 2) * TC + (s + 1) * SUB]
                    return fkh, fqh

                def emit_sprime(h):
                    fkh, fqh = head_slices(h)
                    pss = ps_s0 if h % 2 == 0 else ps_s1
                    if h >= 2:
                        p.w("tensor", sv, sm_done[h - 2])
                    p.o("tensor", lambda E, fkh=fkh, fqh=fqh, pss=pss:
                        E.matmul(pss[:, :], fkh, fqh, start=True, stop=True),
                        st)
                    s_mm[h] = st.n

                emit_sprime(0)
                for h in range(HEADS):
                    hp = h % 2
                    fkh, fqh = head_slices(h)
                    pss = ps_s0 if h % 2 == 0 else ps_s1
                    p.w("vector", st, s_mm[h])
                    if h >= 2:
                        p.w("vector", st, num_done[h - 2])
                    p.o("vector", lambda E, hp=hp, pss=pss: E.tensor_mul(
                        slc(smb, hp, SUB), pss[:, :], cons["cmask_bf"][:, :]),
                        sv)
                    sm_done[h] = sv.n
                    if h + 1 < HEADS:
                        emit_sprime(h + 1)
                    p.w("tensor", sv, sm_done[h])
                    p.o("tensor", lambda E, hp=hp, h=h, s=s: E.matmul(
                        ps_den[:, h:h + 1], slc(smb, hp, SUB),
                        cons["ones_col_bf"][:, :], start=True, stop=False))
                    p.o("tensor", lambda E, fqh=fqh, h=h: E.matmul(
                        ps_den[:, h:h + 1], fqh,
                        ksb[64 * (h % 2):64 * (h % 2) + 64, h // 2:h // 2 + 1],
                        start=False, stop=True))
                    p.o("tensor", lambda E, hp=hp, h=h, s=s: E.matmul(
                        ps_A[:, h * DH:(h + 1) * DH], slc(smb, hp, SUB),
                        vtm[:, s * D + h * DH: s * D + (h + 1) * DH],
                        start=True, stop=False))
                    p.o("tensor", lambda E, fqh=fqh, h=h: E.matmul(
                        ps_A[:, h * DH:(h + 1) * DH], fqh,
                        kvb[64 * (h % 2):64 * (h % 2) + 64,
                            (h // 2) * DH:(h // 2 + 1) * DH],
                        start=False, stop=True), st)
                    num_done[h] = st.n
                heads_done = st.n
                fk_rd = heads_done
                for h in range(HEADS):
                    p.o("tensor", lambda E, h=h, s=s: E.matmul(
                        ps_kv[64 * (h % 2):64 * (h % 2) + 64,
                              (h // 2) * DH:(h // 2 + 1) * DH],
                        ktm[:, s * D + h * DH: s * D + (h + 1) * DH],
                        vtm[:, s * D + h * DH: s * D + (h + 1) * DH],
                        start=first_kv, stop=True, skip_group_check=True,
                        tile_position=(0, 64 * (h % 2))),
                        st if h == HEADS - 1 else None)
                first_kv = False
                kvmm_done = st.n
                p.w("vector", st, heads_done)
                for m in range(NDT):
                    p.o("vector", lambda E, m=m, s=s: E.tensor_reduce(
                        kst[:, m:m + 1],
                        fk[:, m * TC + s * SUB: m * TC + (s + 1) * SUB],
                        AX.X, ALU.add))
                p.o("vector", lambda E: E.tensor_single_scalar(
                    rall[:, :], ps_den[:, :], 1e-6, ALU.max))
                p.o("vector", lambda E: E.tensor_add(
                    ksb[:, :], ksf[:, :], kst[:, :]))
                p.o("vector", lambda E: E.tensor_add(
                    ksf[:, :], ksf[:, :], kst[:, :]))
                p.o("vector", lambda E: E.reciprocal(rinv[:, :], rall[:, :]))
                p.w("vector", st, kvmm_done)
                p.o("vector", lambda E: E.tensor_copy(kvb[:, :], ps_kv[:, :]),
                    sv)
                kv_ev[gsub + 1] = sv.n
                if "den_dbg" in dt:
                    p.w("gpsimd", sv, kv_ev[gsub + 1])
                    dden = p.dma("gpsimd",
                                 dt["den_dbg"][gsub * SUB:(gsub + 1) * SUB, :],
                                 rall[:, :], dm)
                for h in range(HEADS):
                    p.o("vector", lambda E, h=h: E.tensor_scalar_mul(
                        aos[:, h * DH:(h + 1) * DH],
                        ps_A[:, h * DH:(h + 1) * DH], rinv[:, h:h + 1]),
                        sv if h == HEADS - 1 else None)
                a_rd = sv.n
                aos_done = sv.n
                if "aos_dbg" in dt:
                    p.w("gpsimd", sv, aos_done)
                    aosd = p.dma("gpsimd",
                                 dt["aos_dbg"][gsub * SUB:(gsub + 1) * SUB, :],
                                 aos[:, :], dm)
                    p.w("vector", dm, aosd)
                p.w("tensor", sv, aos_done)
                for m in range(NDT):
                    pst = ps_t0 if m % 2 == 0 else ps_t1
                    if m >= 2:
                        p.w("tensor", sv, tr_ev[NDT * NSUB + m - 2])
                    p.o("tensor", lambda E, m=m, pst=pst: E.transpose(
                        pst[:, :], aos[:, m * 128:(m + 1) * 128],
                        cons["ident_bf"][:, :]), st)
                    p.w("vector", st)
                    p.o("vector", lambda E, m=m, s=s, pst=pst: E.tensor_copy(
                        aoT[:, m * TC + s * SUB: m * TC + (s + 1) * SUB],
                        pst[:, :]), sv)
                    tr_ev[NDT * NSUB + m] = sv.n
                aot_done = sv.n
            if "aot_dbg" in dt:
                p.w("gpsimd", sv, aot_done)
                ad = p.dma("gpsimd",
                           dt["aot_dbg"][:, sc * NDT * TC:(sc + 1) * NDT * TC],
                           aoT[:, :], dm)
                if sc == 0 and "wo_dbg" in dt:
                    p.dma("gpsimd", dt["wo_dbg"][:, :], wo[:, :], dm)
            p.w("tensor", sv, aot_done)
            wo_ev = {}
            wo_sv = {}
            for nt in range(NDT):
                pp = nt % 2
                if nt >= 2:
                    p.w("tensor", sv, wo_ev[nt - 2])
                    p.w("tensor", ss, wo_sv[nt - 2])
                for k in range(NDT):
                    p.o("tensor", lambda E, nt=nt, k=k, pp=pp: E.matmul(
                        slc(ps_A, pp, TC),
                        wo[:, k * D + nt * 128: k * D + (nt + 1) * 128],
                        slc(aoT, k, TC), start=(k == 0), stop=(k == NDT - 1)),
                        st if k == NDT - 1 else None)
                p.w("scalar", st)
                if nt >= 1:
                    p.w("scalar", sv, wo_ev[nt - 1])
                p.o("scalar", lambda E, nt=nt, pp=pp: E.activation(
                    tq1[:, :], slc(ps_A, pp, TC), AF.Identity,
                    bias=bos[:, nt:nt + 1]), ss)
                wo_sv[nt] = ss.n
                p.w("vector", ss)
                if sc >= 1 and nt == 0:
                    p.w("vector", do, outd[sc - 1])
                p.o("vector", lambda E, nt=nt, xoff=xoff: E.tensor_add(
                    slc(out_c, nt, TC),
                    x1b[:, xoff + nt * TC: xoff + (nt + 1) * TC],
                    tq1[:, :]), sv)
                wo_ev[nt] = sv.n
            a_wr = st.n
            a_rd = sv.n
            a_rd_ss = ss.n
            xread[sc] = sv.n
            p.w("scalar", sv)
            outd[sc] = p.dma(
                "scalar", x2_r[:, :, sc * TC:(sc + 1) * TC],
                out_c.rearrange("p (a t) -> p a t", a=NDT), do)
            deps["x2_c"][sc] = outd[sc]
        p.w("gpsimd", do)
        p.emit(nc)


def host_inputs(inputs, core):
    f32 = np.float32

    def t2(a):
        n = a.shape[0] // 128
        return np.ascontiguousarray(
            a.reshape(n, 128, -1).transpose(1, 0, 2).reshape(128, -1))

    x = inputs["L0_attn_a_x"][core]
    d = {
        "xin": np.ascontiguousarray(x.T).astype(f32),
        "w_in_blk": np.ascontiguousarray(
            inputs["L0_attn_a_W_in"].reshape(8, 128, 32, 128)
            .transpose(2, 1, 0, 3).reshape(32 * 128, 8 * 128)),
        "w_delta_blk": np.ascontiguousarray(
            inputs["L0_attn_a_W_delta"].reshape(16, 128, 16, 128)
            .transpose(2, 1, 0, 3).reshape(16 * 128, 16 * 128)),
        "w_out_blk": np.ascontiguousarray(
            inputs["L0_attn_a_W_out"].reshape(16, 128, 8, 128)
            .transpose(2, 1, 0, 3).reshape(8 * 128, 16 * 128)),
        "w_b_t": t2(inputs["L0_attn_a_W_B"]),
        "w_c_t": t2(inputs["L0_attn_a_W_C"]),
        "a_log_t": t2(inputs["L0_attn_a_A_log"]),
        "b_delta_t": np.ascontiguousarray(
            inputs["L0_attn_a_b_delta"].reshape(16, 128).T),
        "d_param_t": np.ascontiguousarray(
            inputs["L0_attn_a_D_param"].reshape(16, 128).T),
        "w1_t": np.ascontiguousarray(inputs["L0_ffn_w1"].T),
        "w2_t": np.ascontiguousarray(inputs["L0_ffn_w2"].T),
        "b1_t": np.ascontiguousarray(inputs["L0_ffn_b1"].reshape(32, 128).T),
        "b2_t": np.ascontiguousarray(inputs["L0_ffn_b2"].reshape(8, 128).T),
        "wq_t": np.ascontiguousarray(inputs["L0_attn_b_Wq"].T),
        "wk_t": np.ascontiguousarray(inputs["L0_attn_b_Wk"].T),
        "wv_t": np.ascontiguousarray(inputs["L0_attn_b_Wv"].T),
        "wo_t": np.ascontiguousarray(inputs["L0_attn_b_Wo"].T),
        "bq_t": np.ascontiguousarray(inputs["L0_attn_b_bq"].reshape(8, 128).T),
        "bk_t": np.ascontiguousarray(inputs["L0_attn_b_bk"].reshape(8, 128).T),
        "bv_row": np.ascontiguousarray(inputs["L0_attn_b_bv"].reshape(1, D)),
        "bo_t": np.ascontiguousarray(inputs["L0_attn_b_bo"].reshape(8, 128).T),
        "dropm": _dropmask_bf16()[core],
    }
    return d


_DROPM = None


def _threefry2x32(k0, k1, x0, x1):
    rot = (13, 15, 26, 6, 17, 29, 16, 24)
    x0 = x0.astype(np.uint32).copy()
    x1 = x1.astype(np.uint32).copy()
    ks = (np.uint32(k0), np.uint32(k1),
          np.uint32(k0 ^ k1 ^ np.uint32(0x1BD11BDA)))
    x0 += ks[0]
    x1 += ks[1]
    for i in range(5):
        for r in (rot[:4] if i % 2 == 0 else rot[4:]):
            x0 += x1
            x1 = ((x1 << np.uint32(r)) | (x1 >> np.uint32(32 - r))).astype(
                np.uint32)
            x1 ^= x0
        x0 += ks[(i + 1) % 3]
        x1 += ks[(i + 2) % 3] + np.uint32(i + 1)
    return x0, x1


def _dropmask_bf16():
    global _DROPM
    if _DROPM is None:
        n = 8 * T * D
        io = np.arange(n, dtype=np.uint64)
        c1 = (io >> np.uint64(32)).astype(np.uint32)
        c2 = (io & np.uint64(0xFFFFFFFF)).astype(np.uint32)
        b1, b2 = _threefry2x32(0, 42, c1, c2)
        bits = b1 ^ b2
        u = ((bits >> np.uint32(9)) | np.uint32(0x3F800000)).view(
            np.float32) - 1.0
        keep = (u < np.float32(0.8)).reshape(8, T, D)
        m = (keep.astype(np.float32) * 1.25).astype(BF)
        _DROPM = [np.ascontiguousarray(m[c].T) for c in range(8)]
    return _DROPM


def kernel(**inputs):
    nc, _ = build_program()
    in_maps = [host_inputs(inputs, core) for core in range(8)]
    res = run_bass_kernel_spmd(nc, in_maps, core_ids=list(range(8)))
    outs = [np.ascontiguousarray(r["out"].T) for r in res.results]
    return np.stack(outs, axis=0).astype(np.float32)
